# version 18
# baseline (speedup 1.0000x reference)
import numpy as np
import ml_dtypes
from contextlib import ExitStack

import concourse.bass as bass
import concourse.mybir as mybir
from concourse.bass_utils import run_bass_kernel_spmd

F32 = mybir.dt.float32
BF16 = mybir.dt.bfloat16
I32 = mybir.dt.int32
AF = mybir.ActivationFunctionType
AL = mybir.AluOpType
AX = mybir.AxisListType

ENG = ("sync", "scalar", "vector", "tensor", "gpsimd")

B_PER_CORE = 2
T = 2048
D = 1024
H_LOC = 2
NDC = 8
HID = 3072
NHC = 24


class Sem:
    __slots__ = ("h", "count", "name")

    def __init__(self, h, name):
        self.h = h
        self.count = 0
        self.name = name


class Tok:
    __slots__ = ("sem", "val", "eng", "dma")

    def __init__(self, sem, val, eng, dma):
        self.sem = sem
        self.val = val
        self.eng = eng
        self.dma = dma


def _flat(waits):
    out = []
    for t in waits:
        if t is None:
            continue
        if isinstance(t, (list, tuple)):
            out.extend(_flat(t))
        else:
            out.append(t)
    return out


class W:
    def __init__(self, nc, ctx):
        self.nc = nc
        self.ctx = ctx
        self.ops = {e: [] for e in ENG}
        self.nsem = 0
        self.barrier = []
        self.fence_sem = self.sem("fence")
        self.qsem = {e: self.sem("q_" + e) for e in ("sync", "scalar", "gpsimd")}
        self.dma_sems = {e: [self.qsem[e]] for e in ("sync", "scalar", "gpsimd")}
        self.csem = {}
        self.phase_ops = {e: 0 for e in ENG}

    def sem(self, name):
        h = self.ctx.enter_context(self.nc.semaphore(name))
        return Sem(h, name)

    def dsem(self, name, q, n=1):
        out = []
        for i in range(n):
            s = self.sem("%s%d" % (name, i))
            self.dma_sems[q].append(s)
            out.append(s)
        return out if n > 1 else out[0]

    def op(self, eng, fn, waits=(), inc=None, dma=False):
        wl = {}
        for t in _flat(waits):
            k = id(t.sem)
            v = t.sem.count if t.dma else t.val
            if k not in wl or wl[k][1] < v:
                wl[k] = (t.sem, v)
        for t in self.barrier:
            k = id(t.sem)
            if k not in wl or wl[k][1] < t.val:
                wl[k] = (t.sem, t.val)
        n = 16 if dma else 1
        tok = None
        if inc is not None:
            inc.count += n
            tok = Tok(inc, inc.count, eng, dma)
        self.ops[eng].append((fn, list(wl.values()), inc, n))
        self.phase_ops[eng] += 1
        return tok

    def dma(self, q, out, in_, waits=(), slow=False, sem=None):
        return self.op(q, (lambda e, o=out, i=in_, s=slow:
                           e.dma_start(out=o, in_=i, allow_slow_non_contiguous=s)),
                       waits, inc=(sem or self.qsem[q]), dma=True)

    def fence(self):
        toks = []
        for e in ENG:
            if self.phase_ops[e] == 0:
                continue
            w = []
            if e in self.dma_sems:
                for q in self.dma_sems[e]:
                    if q.count:
                        w.append(Tok(q, q.count, e, True))
            cs = self.csem.get(e)
            if cs is not None and cs.count:
                w.append(Tok(cs, cs.count, e, True))
            toks.append(self.op(e, (lambda en: en.nop()), w, inc=self.fence_sem))
        self.barrier = toks
        for e in ENG:
            self.phase_ops[e] = 0

    def emit(self):
        nc = self.nc
        with nc.Block() as block:
            for e in ENG:
                def runner(eng, e=e):
                    mark = {}
                    for fn, waits, inc, n in self.ops[e]:
                        for sem, v in waits:
                            k = id(sem)
                            if mark.get(k, 0) >= v:
                                continue
                            eng.wait_ge(sem.h, v)
                            mark[k] = v
                        ins = fn(eng)
                        if inc is not None:
                            ins.then_inc(inc.h, n)
                getattr(block, e)(runner)


def A(t, off, dims):
    return bass.AP(t, off, [list(d) for d in dims])


def build_program():
    nc = bass.Bass("TRN2", target_bir_lowering=False)

    x_in = nc.dram_tensor("x", [B_PER_CORE, T, D], F32, kind="ExternalInput")
    wq_in = nc.dram_tensor("wq", [D, D], F32, kind="ExternalInput")
    wk_in = nc.dram_tensor("wk", [D, D], F32, kind="ExternalInput")
    wv_in = nc.dram_tensor("wv", [D, D], F32, kind="ExternalInput")
    wo_in = nc.dram_tensor("wo", [D, D], F32, kind="ExternalInput")
    bq_in = nc.dram_tensor("bq", [D], F32, kind="ExternalInput")
    bk_in = nc.dram_tensor("bk", [D], F32, kind="ExternalInput")
    bv_in = nc.dram_tensor("bv", [D], F32, kind="ExternalInput")
    bo_in = nc.dram_tensor("bo", [D], F32, kind="ExternalInput")
    gamma_in = nc.dram_tensor("gamma", [D], F32, kind="ExternalInput")
    beta_in = nc.dram_tensor("beta", [D], F32, kind="ExternalInput")
    wg_in = nc.dram_tensor("wg", [HID, D], I32, kind="ExternalInput")
    wu_in = nc.dram_tensor("wu", [HID, D], I32, kind="ExternalInput")
    wd_in = nc.dram_tensor("wd", [D, HID], I32, kind="ExternalInput")
    tri_in = nc.dram_tensor("tri", [128, 128], F32, kind="ExternalInput")
    mkb_in = nc.dram_tensor("maskbig", [128, 128], BF16, kind="ExternalInput")
    idb_in = nc.dram_tensor("identb", [128, 128], BF16, kind="ExternalInput")
    idf_in = nc.dram_tensor("identf", [128, 128], F32, kind="ExternalInput")
    y_out = nc.dram_tensor("y", [B_PER_CORE, T, D], F32, kind="ExternalOutput")

    scr_wblk = nc.dram_tensor("scr_wblk", [3, NHC, 8, 128, 128], BF16)
    scr_whilo = nc.dram_tensor("scr_whilo", [4, 2, D, D], BF16)
    scr_o = nc.dram_tensor("scr_o", [2, B_PER_CORE, D, T], BF16)

    ctx = ExitStack()
    SB = lambda n, s, d: ctx.enter_context(nc.sbuf_tensor(n, s, d))
    PS = lambda n, s, d: ctx.enter_context(nc.psum_tensor(n, s, d))

    AA = SB("arena_a", [128, 16 * T], BF16)
    AS = SB("arena_s", [128, 16 * 1024], F32)
    PF = SB("pool_f", [128, 8192], F32)
    PB = SB("pool_b", [128, 20480], BF16)
    tri_s = SB("tri_s", [128, 128], F32)
    mkb_s = SB("mkb_s", [128, 128], BF16)
    idb_s = SB("idb_s", [128, 128], BF16)
    idf_s = SB("idf_s", [128, 128], F32)
    bcols = SB("bcols", [128, 24], F32)
    tiny = SB("tiny", [128, 16], F32)
    trow = SB("trow", [1, 16], F32)
    ones_b = SB("ones_b", [128, 1], BF16)

    pb = [PS("pb%d" % i, [128, 512], F32) for i in range(6)]
    pbt2 = [PS("pbt", [128, 1024], BF16), PS("pbt2", [128, 1024], BF16)]
    pbt = pbt2[0]

    w = W(nc, ctx)
    ld_const = w.dsem("ld_const", "sync")
    ld_wt = w.dsem("ld_wt", "gpsimd", 3)
    st_stg = w.dsem("st_stg", "scalar")
    ld_wof = w.dsem("ld_wof", "gpsimd", 2)
    st_whl = w.dsem("st_whl", "scalar", 2)
    ld_x = w.dsem("ld_x", "sync", 2)
    ld_wsl = w.dsem("ld_wsl", "sync")
    st_osp = w.dsem("st_osp", "scalar")
    ld_wo = w.dsem("ld_wo", "sync")
    ld_ot = w.dsem("ld_ot", "sync", 4)
    ld_vec = w.dsem("ld_vec", "sync")
    ld_wg = w.dsem("ld_wg", "sync", 2)
    ld_wu = w.dsem("ld_wu", "sync", 2)
    ld_wd = w.dsem("ld_wd", "sync", 2)
    ld_x2 = w.dsem("ld_x2", "sync", 2)
    st_y = w.dsem("st_y", "scalar", 2)

    def xn_hi(dc):
        return AA[:, dc * T:(dc + 1) * T]

    def xn_lo(dc):
        return AA[:, (8 + dc) * T:(9 + dc) * T]

    def s_tile(tt):
        return AS[:, tt * 1024:(tt + 1) * 1024]

    EPS = tiny[:, 8:9]

    c_ve = w.sem("c_ve")
    c_se = w.sem("c_se")
    c_te = w.sem("c_te")
    c_ge = w.sem("c_ge")
    w.csem = {"vector": c_ve, "scalar": c_se, "tensor": c_te, "gpsimd": c_ge}

    ct = []
    ct.append(w.dma("sync", tri_s[:, :], tri_in[:, :], sem=ld_const))
    ct.append(w.dma("sync", mkb_s[:, :], mkb_in[:, :], sem=ld_const))
    ct.append(w.dma("sync", idb_s[:, :], idb_in[:, :], sem=ld_const))
    ct.append(w.dma("sync", idf_s[:, :], idf_in[:, :], sem=ld_const))
    for i, bt in enumerate((bq_in, bk_in, bv_in)):
        ct.append(w.dma("sync", bcols[:, i * 8:(i + 1) * 8],
                        A(bt, 0, [[1, 128], [128, 8]]), slow=True, sem=ld_const))
    ct.append(w.op("vector", lambda e: e.memset(tiny[:, 8:9], 1e-5), inc=c_ve))
    ct.append(w.op("vector", lambda e: e.memset(ones_b[:, :], 1.0), inc=c_ve))
    w.barrier = ct

    whl_all = []

    def emit_whilo():
        wof = [PF[:, 0:1024], PF[:, 1024:2048]]
        whib = [PB[:, 17408:18432], PB[:, 18432:19456]]
        wlob = [PB[:, 19456:20480], PB[:, 16384:17408]]
        wof_cons = [[], []]
        whl_cons = [[], []]
        for wi, wsrc in ((0, wq_in), (1, wk_in), (2, wv_in), (3, wo_in)):
            for dc in range(8):
                sl = dc % 2
                tok = w.dma("gpsimd", wof[sl][:, :],
                            A(wsrc, dc * 128 * D, [[D, 128], [1, D]]),
                            waits=wof_cons[sl], sem=ld_wof[sl])
                wof_cons[sl] = []
                hi = w.op("scalar",
                          (lambda e, d=whib[sl][:, :], s=wof[sl][:, :]:
                           e.activation(d, s, AF.Copy)),
                          waits=[tok] + whl_cons[sl], inc=c_se)
                lo = w.op("vector",
                          (lambda e, d=wlob[sl][:, :], s=wof[sl][:, :],
                           h=whib[sl][:, :]:
                           e.tensor_sub(d, s, h)),
                          waits=[tok, hi] + whl_cons[sl], inc=c_ve)
                wof_cons[sl] = [lo]
                t1 = w.dma("scalar",
                           A(scr_whilo, (wi * 2 + 0) * D * D + dc * 128 * D,
                             [[D, 128], [1, D]]), whib[sl][:, :], waits=[hi],
                           sem=st_whl[sl])
                t2 = w.dma("scalar",
                           A(scr_whilo, (wi * 2 + 1) * D * D + dc * 128 * D,
                             [[D, 128], [1, D]]), wlob[sl][:, :], waits=[lo],
                           sem=st_whl[sl])
                whl_cons[sl] = [t1, t2]
                whl_all.extend([t1, t2])

    wblk_state = {"units": None, "wtl_cons": None, "stg_cons": None,
                  "pbt_cons": None}

    def _wblk_unit(u):
        wtl = [PB[:, 17408:18432], PB[:, 18432:19456], PB[:, 19456:20480]]
        kind = u[0]
        sl = u[3]
        if kind == "gu":
            _, wi, hc, sl = u
            wsrc = wg_in if wi == 0 else wu_in
            tok = w.dma("gpsimd", wtl[sl][:, :],
                        A(wsrc, hc * 128 * D, [[D, 128], [1, D]]),
                        waits=[wblk_state["wtl_cons"][sl]], sem=ld_wt[sl])
            dst_off = (hc * 2 + wi) * 8 * 16384
            dims = [[128, 128], [16384, 8], [1, 128]]
        else:
            _, oc, g, sl = u
            tok = w.dma("gpsimd", wtl[sl][:, :],
                        A(wd_in, oc * 128 * HID + g * 1024,
                          [[HID, 128], [1, 1024]]),
                        waits=[wblk_state["wtl_cons"][sl]], sem=ld_wt[sl])
            dst_off = 2 * NHC * 8 * 16384 + g * 8 * 8 * 16384 + oc * 16384
            dims = [[128, 128], [8 * 16384, 8], [1, 128]]
        tt = None
        for dc in range(8):
            tt = w.op("tensor",
                      (lambda e, a=pbt2[1][:, dc * 128:(dc + 1) * 128],
                       b=wtl[sl][:, dc * 128:(dc + 1) * 128]:
                       e.transpose(a, b, idb_s[:, :])),
                      waits=[tok, wblk_state["pbt_cons"]] if dc == 0 else [tok],
                      inc=c_te)
        ev = w.op("vector",
                  (lambda e, d=wtl[sl][:, :]:
                   e.tensor_copy(d, pbt2[1][:, 0:1024])),
                  waits=[tt], inc=c_ve)
        wblk_state["pbt_cons"] = ev
        wblk_state["wtl_cons"][sl] = w.dma(
            "scalar", A(scr_wblk, dst_off, dims),
            wtl[sl][:, :].rearrange("p (j jj) -> p j jj", j=8),
            waits=[ev], sem=st_stg)

    def emit_wblk_units(n):
        st = wblk_state
        if st["units"] is None:
            units = []
            for wi in (0, 1):
                for hc in range(NHC):
                    units.append(("gu", wi, hc, (wi * NHC + hc) % 3))
            for oc in range(8):
                for g in range(3):
                    units.append(("wd", oc, g, (48 + oc * 3 + g) % 3))
            st["units"] = units
            st["wtl_cons"] = [whl_all, whl_all, whl_all]
            st["stg_cons"] = None
            st["pbt_cons"] = None
        k = 0
        while st["units"] and k < n:
            _wblk_unit(st["units"].pop(0))
            k += 1

    V2OFF = 3072
    pbk_cons = {i: [] for i in range(6)}
    lrow_cons = [[], []]
    for b in range(B_PER_CORE):
        xt = [PF[:, 2048:3072], PF[:, 3072:4096]]
        sqt = [PF[:, 4096:5120], PF[:, 4096:5120]]
        xnt = [PF[:, 6144:7168], PF[:, 7168:8192]]
        su16 = PF[:, 5120:5136]
        ssq16 = PF[:, 5136:5152]
        negmu16 = PF[:, 5152:5168]
        m216 = PF[:, 5168:5184]
        r16 = PF[:, 5184:5200]
        xt_prod = [None, None]
        xt_cons = [[], []]
        xn_cons = [[], []]
        sq_cons = []
        pbtc = [[]]
        aa_done = {}
        st_toks = []
        for tt in range(16):
            s = tt % 2
            tok = w.dma("sync", xt[s][:, :],
                        A(x_in, (b * T + tt * 128) * D, [[D, 128], [1, D]]),
                        waits=xt_cons[s], sem=ld_x[s])
            xt_cons[s] = []
            sq = w.op("scalar", (lambda e, d=sqt[s], x_=xt[s], tt=tt:
                                 e.activation(d[:, :], x_[:, :], AF.Square,
                                              accum_out=ssq16[:, tt:tt + 1])),
                      waits=[tok] + sq_cons, inc=c_se)
            sq_cons = [sq]
            v1 = w.op("vector", (lambda e, x_=xt[s], tt=tt:
                                 e.reduce_sum(su16[:, tt:tt + 1], x_[:, :],
                                              axis=AX.X)),
                      waits=[tok], inc=c_ve)
            xt_cons[s] = [sq, v1]
            st_toks += [sq, v1]
        v3 = w.op("vector", (lambda e: e.tensor_scalar_mul(negmu16, su16,
                                                           -1.0 / 1024.0)),
                  waits=st_toks, inc=c_ve)
        v4 = w.op("vector", (lambda e: e.tensor_mul(m216, negmu16, negmu16)),
                  waits=[v3], inc=c_ve)
        v5 = w.op("vector", (lambda e: e.scalar_tensor_tensor(
            m216, ssq16, 1.0 / 1024.0, m216, AL.mult, AL.subtract)),
                  waits=[v4] + st_toks, inc=c_ve)
        v5e = w.op("vector", (lambda e: e.tensor_scalar_add(m216, m216, 1e-5)),
                   waits=[v5], inc=c_ve)
        s1 = w.op("scalar", (lambda e: e.activation(r16, m216, AF.Ln)),
                  waits=[v5e], inc=c_se)
        s2 = w.op("scalar", (lambda e: e.activation(r16, r16, AF.Exp, scale=-0.5)),
                  waits=[s1], inc=c_se)
        rb16 = PF[:, 5200:5216]
        s3 = w.op("vector", (lambda e: e.tensor_mul(rb16, negmu16, r16)),
                  waits=[s2], inc=c_ve)
        apend = []

        def flush_atr(item):
            s, tt, v6 = item
            for g2_ in range(2):
                bank = pb[(tt * 2 + g2_) % 6]
                tks = []
                for j in range(4):
                    dc = g2_ * 4 + j
                    tks.append(w.op("tensor",
                                    (lambda e, o=bank[:, j * 128:(j + 1) * 128],
                                     s_=xnt[s][:, dc * 128:(dc + 1) * 128]:
                                     e.transpose(o, s_, idf_s[:, :])),
                                    waits=[v6] + pbk_cons[(tt * 2 + g2_) % 6],
                                    inc=c_te))
                xn_cons[s].append(tks[-1])
                pbk_cons[(tt * 2 + g2_) % 6] = []
                aa3 = AA[:, :].rearrange("p (dc t) -> p dc t", dc=16)
                hi = w.op("scalar",
                          (lambda e, tt=tt, g2_=g2_, bank=bank,
                           d=aa3[:, g2_ * 4:g2_ * 4 + 4, tt * 128:(tt + 1) * 128]:
                           e.activation(d,
                                        bank[:, 0:512].rearrange(
                                            "p (dc j) -> p dc j", dc=4),
                                        AF.Copy)),
                          waits=[tks[-1]], inc=c_se)
                lo = w.op("vector",
                          (lambda e, tt=tt, g2_=g2_, bank=bank,
                           d=aa3[:, 8 + g2_ * 4:8 + g2_ * 4 + 4,
                                 tt * 128:(tt + 1) * 128],
                           h=aa3[:, g2_ * 4:g2_ * 4 + 4, tt * 128:(tt + 1) * 128]:
                           e.tensor_sub(d,
                                        bank[:, 0:512].rearrange(
                                            "p (dc j) -> p dc j", dc=4), h)),
                          waits=[tks[-1], hi], inc=c_ve)
                pbk_cons[(tt * 2 + g2_) % 6] = [hi, lo]
                aa_done.setdefault(tt // 4, []).append(lo)

        for tt in range(16):
            s = tt % 2
            tok = w.dma("sync", xt[s][:, :],
                        A(x_in, (b * T + tt * 128) * D, [[D, 128], [1, D]]),
                        waits=xt_cons[s], sem=ld_x[s])
            xt_cons[s] = []
            v6 = w.op("scalar", (lambda e, d=xnt[s], x_=xt[s], tt=tt:
                                 e.activation(d[:, :], x_[:, :], AF.Identity,
                                              bias=rb16[:, tt:tt + 1],
                                              scale=r16[:, tt:tt + 1])),
                      waits=[tok, s3] + xn_cons[s], inc=c_se)
            xt_cons[s] = [v6]
            xn_cons[s] = []
            apend.append((s, tt, v6))
            if len(apend) > 1:
                flush_atr(apend.pop(0))
        while apend:
            flush_atr(apend.pop(0))
        if b == 0:
            emit_whilo()
        w.fence()

        kqev = [PF[:, 0:512], PF[:, 512:1024]]
        vfm = PF[:, 1024:3072]
        linvb = vfm[:, 0:512]
        V2 = PF[:, V2OFF:V2OFF + 2080]
        attb = [[PF[:, 5152:5664], PF[:, 5664:6176]],
                [PF[:, 6176:6688], PF[:, 6688:7200]]]
        ofst = PF[:, 7200:7712]
        KHI = PB[:, 0:2048]
        KLO = PB[:, 2048:4096]
        QHI = PB[:, 4096:6144]
        QLO = PB[:, 6144:8192]
        WSL = {}
        for i, nm in enumerate(("wqh", "wql", "wkh", "wkl", "wvh", "wvl")):
            WSL[nm] = PB[:, 8192 + i * 1024:8192 + (i + 1) * 1024]
        ksqb = [PB[:, 14336:14848], PB[:, 14848:15360]]
        ohst = [PB[:, 15360 + 512 * i:15360 + 512 * (i + 1)] for i in range(4)]

        NEGM = tiny[:, 0:2]
        LV = tiny[:, 2:4]

        v2cons = []
        m1 = w.op("vector",
                  (lambda e: e.memset(
                      V2.rearrange("p (tt g j) -> p tt g j", g=2, j=65)[:, :, :, 64:65],
                      1.0)), inc=c_ve)
        v2ones = m1

        wsl_prod = {}
        wsl_cons = {}
        pbk_cons = {i: [] for i in range(6)}
        ksq_cons2 = [[]]
        trow_cons2 = [[]]
        evac_pend = [None]
        ofst_cons2 = [[]]
        linvb_cons2 = [[]]
        vfm_cons2 = [[]]
        v2_cons2 = [[]]
        kq_cons = [[], []]
        khql_cons = []
        ksq_cons = [[], []]
        att_cons = [[[], []], [[], []]]
        oh_cons = [[], [], [], []]
        negm_cons = []
        trow_cons = []

        for hg in range(8):
            rd_prev = {}
            for i, (wi, hl, nm) in enumerate(
                    ((0, 0, "wqh"), (0, 1, "wql"), (1, 0, "wkh"),
                     (1, 1, "wkl"), (2, 0, "wvh"), (2, 1, "wvl"))):
                tok = w.dma("sync", WSL[nm][:, :].rearrange("p (dc j) -> p dc j", dc=8),
                            A(scr_whilo, (wi * 2 + hl) * D * D + hg * 128,
                              [[D, 128], [128 * D, 8], [1, 128]]),
                            waits=wsl_cons.get(nm, []) + whl_all, sem=ld_wsl)
                wsl_prod[nm] = tok
                wsl_cons[nm] = []

            mhat_pend = []

            def flush_mhat(item):
                sq, tc, mcol = item
                bq_toks = []
                for hh in range(2):
                    bnk = pb[4 + hh]
                    mt = w.op("tensor",
                              (lambda e, o=bnk[0:1, 0:512],
                               l=ones_b[hh * 64:(hh + 1) * 64, 0:1],
                               r=ksqb[tc % 2][hh * 64:(hh + 1) * 64, :],
                               tp=(hh * 64, 0):
                               e.matmul(o, l, r, start=True, stop=True,
                                        tile_position=tp)),
                              waits=[sq] + pbk_cons[4 + hh], inc=c_te)
                    pbk_cons[4 + hh] = []
                    if (mcol, hh) not in rd_prev:
                        rd = w.op("vector",
                                  (lambda e, d=trow[0:1, mcol + hh:mcol + hh + 1],
                                   s_=bnk[0:1, 0:512]:
                                   e.reduce_max(d, s_, axis=AX.X)),
                                  waits=[mt] + trow_cons2[0], inc=c_ve)
                    else:
                        tc_ = 4 + hh + (0 if mcol == 2 else 2)
                        r1 = w.op("vector",
                                  (lambda e, s_=bnk[0:1, 0:512], c=tc_:
                                   e.reduce_max(trow[0:1, c:c + 1], s_,
                                                axis=AX.X)),
                                  waits=[mt, rd_prev.get((mcol, hh))], inc=c_ve)
                        rd = w.op("vector",
                                  (lambda e, d=trow[0:1, mcol + hh:mcol + hh + 1],
                                   c=tc_:
                                   e.tensor_max(d, d, trow[0:1, c:c + 1])),
                                  waits=[r1], inc=c_ve)
                    rd_prev[(mcol, hh)] = rd
                    pbk_cons[4 + hh].append(rd)
                    bq_toks.append(rd)
                ksq_cons2[0] = bq_toks
                trow_cons2[0] = bq_toks

            for which, (whn, wln, DHI, DLO, bcol0, scl, mcol) in enumerate(
                    (("wkh", "wkl", KHI, KLO, 8, 0.5, 2),
                     ("wqh", "wql", QHI, QLO, 0, 0.25, 0))):
                for tc in range(4):
                    bank = pb[tc % 2]
                    mmt = None
                    first = True
                    for (wn, xs) in ((whn, "h"), (whn, "l"), (wln, "h")):
                        wt = WSL[wn]
                        for dc in range(8):
                            rhs = (xn_hi(dc) if xs == "h" else xn_lo(dc))[:, tc * 512:(tc + 1) * 512]
                            mmt = w.op("tensor",
                                       (lambda e, o=bank[:, 0:512],
                                        l=wt[:, dc * 128:(dc + 1) * 128], r=rhs,
                                        st=first, sp=(wn == wln and dc == 7):
                                        e.matmul(o, l, r, start=st, stop=sp)),
                                       waits=([wsl_prod[wn]] + pbk_cons[tc % 2]
                                              + aa_done.get(tc, []) + khql_cons)
                                       if first else [wsl_prod[wn]],
                                       inc=c_te)
                            first = False
                    wsl_cons.setdefault(whn, []).append(mmt)
                    wsl_cons.setdefault(wln, []).append(mmt)
                    pbk_cons[tc % 2] = []
                    ev = w.op("vector",
                              (lambda e, d=kqev[tc % 2], s_=bank,
                               bc=bcols[:, bcol0 + hg:bcol0 + hg + 1], sc=scl:
                               e.tensor_scalar(d[:, :], s_[:, 0:512], bc, sc,
                                               AL.add, AL.mult)),
                              waits=[mmt] + kq_cons[tc % 2], inc=c_ve)
                    pbk_cons[tc % 2].append(ev)
                    kq_cons[tc % 2] = []
                    sq = w.op("scalar",
                              (lambda e, d=ksqb[tc % 2], s_=kqev[tc % 2]:
                               e.activation(d[:, :], s_[:, :], AF.Square)),
                              waits=[ev] + ksq_cons2[0], inc=c_se)
                    kq_cons[tc % 2].append(sq)
                    hi = w.op("scalar",
                              (lambda e, d=DHI[:, tc * 512:(tc + 1) * 512],
                               s_=kqev[tc % 2]:
                               e.activation(d, s_[:, :], AF.Copy)),
                              waits=[ev], inc=c_se)
                    lo = w.op("vector",
                              (lambda e, d=DLO[:, tc * 512:(tc + 1) * 512],
                               s_=kqev[tc % 2], h=DHI[:, tc * 512:(tc + 1) * 512]:
                               e.tensor_sub(d, s_[:, :], h)),
                              waits=[hi], inc=c_ve)
                    kq_cons[tc % 2].extend([hi, lo])
                    mhat_pend.append((sq, tc, mcol))
                    if len(mhat_pend) > 1:
                        flush_mhat(mhat_pend.pop(0))

            while mhat_pend:
                flush_mhat(mhat_pend.pop(0))

            vpend = []

            def flush_vtr(item):
                tc, ev = item
                bankt = pb[2 + tc % 2]
                tks = []
                for j in range(4):
                    tks.append(w.op("tensor",
                                    (lambda e, o=bankt[:, j * 128:(j + 1) * 128],
                                     s_=vfm[:, (tc * 4 + j) * 128:(tc * 4 + j + 1) * 128]:
                                     e.transpose(o, s_, idf_s[:, :])),
                                    waits=[ev] + pbk_cons[2 + tc % 2], inc=c_te))
                pbk_cons[2 + tc % 2] = []
                vfm_cons2[0].append(tks[-1])
                evv = w.op("vector",
                           (lambda e, tc=tc, s_=bankt:
                            e.tensor_copy(
                                V2.rearrange("p (tt g j) -> p tt g j", g=2, j=65)
                                [:, tc * 4:(tc + 1) * 4, :, 0:64],
                                s_[:, 0:512].rearrange("p (tt g j) -> p tt g j",
                                                       g=2, j=64))),
                           waits=[tks[-1], v2ones] + v2_cons2[0], inc=c_ve)
                v2_cons2[0] = []
                pbk_cons[2 + tc % 2].append(evv)
                v2cons.append(evv)

            for tc in range(4):
                bank = pb[tc % 2]
                mmt = None
                first = True
                for (wn, xs) in (("wvh", "h"), ("wvh", "l"), ("wvl", "h")):
                    wt = WSL[wn]
                    for dc in range(8):
                        rhs = (xn_hi(dc) if xs == "h" else xn_lo(dc))[:, tc * 512:(tc + 1) * 512]
                        mmt = w.op("tensor",
                                   (lambda e, o=bank[:, 0:512],
                                    l=wt[:, dc * 128:(dc + 1) * 128], r=rhs,
                                    st=first, sp=(wn == "wvl" and dc == 7):
                                    e.matmul(o, l, r, start=st, stop=sp)),
                                   waits=[wsl_prod[wn]] + (pbk_cons[tc % 2] if first else []),
                                   inc=c_te)
                        first = False
                wsl_cons.setdefault("wvh", []).append(mmt)
                wsl_cons.setdefault("wvl", []).append(mmt)
                pbk_cons[tc % 2] = []
                ev = w.op("vector",
                          (lambda e, d=vfm[:, tc * 512:(tc + 1) * 512], s_=bank,
                           bc=bcols[:, 16 + hg:17 + hg]:
                           e.tensor_scalar_add(d, s_[:, 0:512], bc)),
                          waits=[mmt] + vfm_cons2[0], inc=c_ve)
                vfm_cons2[0] = []
                pbk_cons[tc % 2].append(ev)
                vpend.append((tc, ev))
                if len(vpend) > 1:
                    flush_vtr(vpend.pop(0))
            while vpend:
                flush_vtr(vpend.pop(0))

            g1 = w.op("vector",
                      (lambda e: e.tensor_add(trow[0:1, 8:10], trow[0:1, 0:2],
                                              trow[0:1, 2:4])),
                      waits=trow_cons2[0] + ksq_cons2[0]
                      + list(rd_prev.values()), inc=c_ve)
            g2 = w.op("vector",
                      (lambda e: e.tensor_scalar(trow[0:1, 8:10], trow[0:1, 8:10],
                                                 -0.505, -0.01, AL.mult, AL.add)),
                      waits=[g1], inc=c_ve)
            g3 = w.op("tensor",
                      (lambda e: e.matmul(pb[5][:, 0:2], tri_s[0:1, 0:128],
                                          trow[0:1, 8:10], start=True, stop=True)),
                      waits=[g2] + pbk_cons[5], inc=c_te)
            g4 = w.op("vector",
                      (lambda e: e.tensor_copy(NEGM, pb[5][:, 0:2])),
                      waits=[g3] + negm_cons, inc=c_ve)
            pbk_cons[5] = [g4]
            negm_tok = g4
            trow_cons2[0] = [g1]
            negm_cons = []

            def emit_evac(av_l, sh, qb_e, rcs):
                mul_toks = []
                for hh in range(2):
                    bank = pb[3 + hh]
                    lrow = vfm[0:1, 1024 + hh * 512:1024 + hh * 512 + 512]
                    rc = rcs[hh]
                    bm = w.op("tensor",
                              (lambda e, d=lrow:
                               e.matmul(pb[5][:, 0:512], tri_s[0:1, 0:128], d,
                                        start=True, stop=True)),
                              waits=[rc] + pbk_cons[5], inc=c_te)
                    cp = w.op("scalar",
                              (lambda e: e.copy(linvb[:, :], pb[5][:, 0:512])),
                              waits=[bm] + linvb_cons2[0], inc=c_se)
                    pbk_cons[5] = [cp]
                    ml = w.op("vector",
                              (lambda e, hh=hh, s_=bank:
                               e.tensor_mul(ofst[hh * 64:(hh + 1) * 64, 0:512],
                                            s_[0:64, 0:512],
                                            linvb[0:64, 0:512])),
                              waits=[cp] + ofst_cons2[0], inc=c_ve)
                    linvb_cons2[0] = [ml]
                    lrow_cons[hh] = [bm]
                    pbk_cons[3 + hh] = [ml]
                    mul_toks.append(ml)
                ofst_cons2[0] = []
                hi = w.op("scalar",
                          (lambda e, d=ohst[sh]: e.activation(d[:, :], ofst[:, :],
                                                              AF.Copy)),
                          waits=mul_toks + oh_cons[sh], inc=c_se)
                lo = w.op("vector",
                          (lambda e, d=ohst[2 + sh], h=ohst[sh]:
                           e.tensor_sub(d[:, :], ofst[:, :], h[:, :])),
                          waits=[hi] + oh_cons[2 + sh], inc=c_ve)
                ofst_cons2[0] = [lo]
                d1 = w.dma("scalar",
                           A(scr_o, (b * D + hg * 128) * T + qb_e * 512,
                             [[T, 128], [1, 512]]),
                           ohst[sh][:, :], waits=[hi], sem=st_osp)
                d2 = w.dma("scalar",
                           A(scr_o, ((B_PER_CORE + b) * D + hg * 128) * T + qb_e * 512,
                             [[T, 128], [1, 512]]),
                           ohst[2 + sh][:, :], waits=[lo], sem=st_osp)
                oh_cons[sh] = [d1]
                oh_cons[2 + sh] = [d2]

            for qb in range(4):
                nsc = 4 * qb + 4
                exp_toks = {}
                mask_toks = {}
                av_last = {}

                def emit_scores(sc):
                    s0 = sc * 128
                    qlo = max(qb * 512, s0)
                    N = qb * 512 + 512 - qlo
                    toks = {}
                    for hh in range(2):
                        bank = pb[(2 * sc + hh) % 3]
                        diag = s0 >= qb * 512
                        mmt = None
                        first = True
                        for KX, QX in ((KHI, QHI), (KHI, QLO), (KLO, QHI)):
                            mmt = w.op("tensor",
                                       (lambda e, o=bank[:, 0:N],
                                        l=KX[hh * 64:(hh + 1) * 64, s0:s0 + 128],
                                        r=QX[hh * 64:(hh + 1) * 64, qlo:qlo + N],
                                        st=first, sp=(KX is KLO and not diag),
                                        tp=(hh * 64, 0):
                                        e.matmul(o, l, r, start=st, stop=sp,
                                                 tile_position=tp)),
                                       waits=(pbk_cons[(2 * sc + hh) % 3]
                                              + khql_cons_local) if first else [],
                                       inc=c_te)
                            first = False
                        if diag:
                            mmt = w.op("tensor",
                                       (lambda e, o=bank[:, 0:128]:
                                        e.matmul(o, idb_s[:, :], mkb_s[:, :],
                                                 start=False, stop=True)),
                                       inc=c_te)
                        pbk_cons[(2 * sc + hh) % 3] = []
                        toks[hh] = mmt
                    return toks, qlo, N

                def emit_exp(sc, toks, qlo, N):
                    s0 = sc * 128
                    slot = sc % 2
                    for hh in range(2):
                        bank = pb[(2 * sc + hh) % 3]
                        ex = w.op("scalar",
                                  (lambda e, d=attb[slot][hh][:, 0:N],
                                   s_=bank[:, 0:N], bc=NEGM[:, hh:hh + 1]:
                                   e.activation(d, s_, AF.Exp, bias=bc)),
                                  waits=[toks[hh], negm_tok] + att_cons[slot][hh],
                                  inc=c_se)
                        att_cons[slot][hh] = []
                        pbk_cons[(2 * sc + hh) % 3].append(ex)
                        exp_toks[(sc, hh)] = ex

                def emit_av(sc):
                    s0 = sc * 128
                    qlo = max(qb * 512, s0)
                    N = qb * 512 + 512 - qlo
                    qoff = qlo - qb * 512
                    slot = sc % 2
                    for hh in range(2):
                        bank = pb[3 + hh]
                        wts = [exp_toks[(sc, hh)]]
                        if sc == 0:
                            wts += pbk_cons[3 + hh] + v2cons_w
                        mmt = w.op("tensor",
                                   (lambda e, o=bank[0:65, qoff:qoff + N],
                                    l=V2[:, sc * 130 + hh * 65:sc * 130 + hh * 65 + 65],
                                    r=attb[slot][hh][:, 0:N],
                                    st=(sc == 0), sp=(sc == nsc - 1):
                                    e.matmul(o, l, r, start=st, stop=sp)),
                                   waits=wts, inc=c_te)
                        att_cons[slot][hh].append(mmt)
                        av_last[hh] = mmt
                    if sc == 0:
                        pbk_cons[3] = []
                        pbk_cons[4] = []

                khql_cons_local = kq_cons[0] + kq_cons[1]
                v2cons_w = list(v2cons)
                prev = None
                for sc in range(nsc):
                    tks, qlo, N = emit_scores(sc)
                    emit_exp(sc, tks, qlo, N)
                    if sc == 0 and evac_pend[0] is not None:
                        emit_evac(*evac_pend[0])
                        evac_pend[0] = None
                    if prev is not None:
                        emit_av(prev)
                    prev = sc
                emit_av(prev)
                if b == 0:
                    emit_wblk_units(2)
                rcs = {}
                for hh in range(2):
                    lrow = vfm[0:1, 1024 + hh * 512:1024 + hh * 512 + 512]
                    rc0 = w.op("scalar",
                               (lambda e, d=lrow, s_=pb[3 + hh]:
                                e.activation(d, s_[64:65, 0:512], AF.Ln)),
                               waits=[av_last[hh]] + lrow_cons[hh], inc=c_se)
                    rcs[hh] = w.op("scalar",
                                   (lambda e, d=lrow:
                                    e.activation(d, d, AF.Exp, scale=-1.0)),
                                   waits=[rc0], inc=c_se)
                evac_pend[0] = (dict(av_last), qb % 2, qb, rcs)

            if evac_pend[0] is not None:
                emit_evac(*evac_pend[0])
                evac_pend[0] = None
            if b == 0 and hg == 7:
                emit_wblk_units(72)

            khql_cons = [t for hh in range(2) for sl2 in range(2)
                         for t in att_cons[sl2][hh]]
            v2_cons2[0] = [t for hh in range(2) for sl2 in range(2)
                           for t in att_cons[sl2][hh]]
            v2cons = []
            vfm_cons2[0] = vfm_cons2[0] + lrow_cons[0] + lrow_cons[1] + linvb_cons2[0]
            negm_cons = [exp_toks[k] for k in exp_toks][-1:]
        w.fence()

        gamb = PF[:, 0:1024]
        betb = PF[:, 1024:2048]
        bob = PF[:, 2048:3072]
        sqc = PF[:, 3072:4096]
        statc16 = PF[:, 4096:4176]
        ofm_hi = [PB[:, 4096:6144], PB[:, 6144:8192]]
        ofm_lo = [PB[:, 8192:10240], PB[:, 10240:12288]]
        woh = PB[:, 12288:16384]
        wol = PB[:, 16384:20480]

        bc_toks = []
        ld_toks = []
        for vin, dst in ((gamma_in, gamb), (beta_in, betb), (bo_in, bob)):
            ld_toks.append(w.dma("sync", dst[0:1, :],
                                 A(vin, 0, [[1024, 1], [1, 1024]]), sem=ld_vec))
        for vi, (vin, dst) in enumerate(((gamma_in, gamb), (beta_in, betb),
                                         (bo_in, bob))):
            cts = []
            for half in range(2):
                bk = pb[4 + half]
                mmb = w.op("tensor",
                           (lambda e, d=dst, hf=half, b_=bk:
                            e.matmul(b_[:, 0:512], tri_s[0:1, 0:128],
                                     d[0:1, hf * 512:(hf + 1) * 512],
                                     start=True, stop=True)),
                           waits=ld_toks + pbk_cons[4 + half], inc=c_te)
                cp = w.op("vector",
                          (lambda e, d=dst, hf=half, b_=bk:
                           e.tensor_copy(d[:, hf * 512:(hf + 1) * 512],
                                         b_[:, 0:512])),
                          waits=[mmb], inc=c_ve)
                pbk_cons[4 + half] = [cp]
                cts.append(cp)
            bc_toks.append(cts)
        ofm_cons = []
        wo_cons = []
        y_cons = {}
        as_toks = []
        for oh in range(2):
            w1 = w.dma("sync", woh[:, :].rearrange("p (dc j) -> p dc j", dc=8),
                       A(scr_whilo, 6 * D * D + oh * 512,
                         [[D, 128], [128 * D, 8], [1, 512]]),
                       waits=wo_cons + whl_all, sem=ld_wo)
            w2 = w.dma("sync", wol[:, :].rearrange("p (dc j) -> p dc j", dc=8),
                       A(scr_whilo, 7 * D * D + oh * 512,
                         [[D, 128], [128 * D, 8], [1, 512]]),
                       waits=wo_cons + whl_all, sem=ld_wo)
            wo_cons = []
            ofm_slot_cons = [list(ofm_cons), list(ofm_cons)]
            for tg8 in range(8):
                sl8 = tg8 % 2
                l1 = w.dma("sync",
                           ofm_hi[sl8][:, :].rearrange("p (dc j) -> p dc j", dc=8),
                           A(scr_o, (b * D) * T + tg8 * 256,
                             [[T, 128], [128 * T, 8], [1, 256]]),
                           waits=ofm_slot_cons[sl8], sem=ld_ot[sl8])
                l2 = w.dma("sync",
                           ofm_lo[sl8][:, :].rearrange("p (dc j) -> p dc j", dc=8),
                           A(scr_o, ((B_PER_CORE + b) * D) * T + tg8 * 256,
                             [[T, 128], [128 * T, 8], [1, 256]]),
                           waits=ofm_slot_cons[sl8], sem=ld_ot[2 + sl8])
                ofm_slot_cons[sl8] = []
                for t4 in range(2):
                    tt = tg8 * 2 + t4
                    bank = pb[tt % 4]
                    mmt = None
                    first = True
                    for OX, WX, wtok, ltok in (
                            (ofm_hi[sl8], woh, w1, l1), (ofm_hi[sl8], wol, w2, l1),
                            (ofm_lo[sl8], woh, w1, l2)):
                        lo_last = OX is ofm_lo[sl8]
                        for dc in range(8):
                            mmt = w.op("tensor",
                                       (lambda e, o=bank[:, 0:512],
                                        l=OX[:, dc * 256 + t4 * 128:dc * 256 + t4 * 128 + 128],
                                        r=WX[:, dc * 512:(dc + 1) * 512],
                                        st=first,
                                        sp=(lo_last and dc == 7):
                                        e.matmul(o, l, r, start=st, stop=sp)),
                                       waits=([l1, l2, wtok] + pbk_cons[tt % 4])
                                       if first else [wtok, ltok],
                                       inc=c_te)
                            first = False
                    pbk_cons[tt % 4] = []
                    ofm_slot_cons[sl8].append(mmt)
                    ev = w.op("vector",
                              (lambda e, tt=tt, oh=oh, s_=bank:
                               e.tensor_add(s_tile(tt)[:, oh * 512:(oh + 1) * 512],
                                            s_[:, 0:512],
                                            bob[:, oh * 512:(oh + 1) * 512])),
                              waits=[mmt, bc_toks[2]] + y_cons.get(tt, []),
                              inc=c_ve)
                    pbk_cons[tt % 4].append(ev)
                    y_cons.setdefault(tt, []).append(ev)
            ofm_cons = ofm_slot_cons[0] + ofm_slot_cons[1]
            wo_cons = list(ofm_cons)
        suc = statc16[:, 0:16]
        ssqc = statc16[:, 16:32]
        negmuc = statc16[:, 32:48]
        m2c = statc16[:, 48:64]
        rc16 = statc16[:, 64:80]
        sq_cons = []
        stc_toks = []
        for tt in range(16):
            yt = s_tile(tt)
            sq = w.op("scalar", (lambda e, a=yt, tt=tt:
                                 e.activation(sqc[:, :], a[:, :], AF.Square,
                                              accum_out=ssqc[:, tt:tt + 1])),
                      waits=y_cons.get(tt, []) + sq_cons, inc=c_se)
            sq_cons = [sq]
            v1 = w.op("vector", (lambda e, a=yt, tt=tt:
                                 e.reduce_sum(suc[:, tt:tt + 1], a[:, :],
                                              axis=AX.X)),
                      waits=y_cons.get(tt, []), inc=c_ve)
            stc_toks += [sq, v1]
        v3 = w.op("vector", (lambda e: e.tensor_scalar_mul(negmuc, suc,
                                                           -1.0 / 1024.0)),
                  waits=stc_toks, inc=c_ve)
        v4 = w.op("vector", (lambda e: e.tensor_mul(m2c, negmuc, negmuc)),
                  waits=[v3], inc=c_ve)
        v5 = w.op("vector", (lambda e: e.scalar_tensor_tensor(
            m2c, ssqc, 1.0 / 1024.0, m2c, AL.mult, AL.subtract)),
                  waits=[v4] + stc_toks, inc=c_ve)
        v5e = w.op("vector", (lambda e: e.tensor_scalar_add(m2c, m2c, 1e-5)),
                   waits=[v5], inc=c_ve)
        s1 = w.op("scalar", (lambda e: e.activation(rc16, m2c, AF.Ln)),
                  waits=[v5e], inc=c_se)
        s2 = w.op("scalar", (lambda e: e.activation(rc16, rc16, AF.Exp,
                                                    scale=-0.5)),
                  waits=[s1], inc=c_se)
        for tt in range(16):
            yt = s_tile(tt)
            v6 = w.op("vector", (lambda e, a=yt, tt=tt:
                                 e.tensor_scalar(a[:, :], a[:, :],
                                                 negmuc[:, tt:tt + 1],
                                                 rc16[:, tt:tt + 1],
                                                 AL.add, AL.mult)),
                      waits=[s2], inc=c_ve)
            v7 = w.op("vector", (lambda e, a=yt:
                                 e.tensor_mul(a[:, :], a[:, :], gamb[:, :])),
                      waits=[v6, bc_toks[0]], inc=c_ve)
            v8 = w.op("vector", (lambda e, a=yt:
                                 e.tensor_add(a[:, :], a[:, :], betb[:, :])),
                      waits=[v7, bc_toks[1]], inc=c_ve)
            as_toks.append(v8)
        w.fence()

        sil = [PF[:, 0:256], PF[:, 256:512]]
        hfl = [PF[:, 512:768], PF[:, 768:1024]]
        gt = PF[:, 1024:2048]
        xt2 = [PF[:, 2048:3072], PF[:, 3072:4096]]
        outt = [PF[:, 4096:5120], PF[:, 5120:6144]]
        hact_hi = PB[:, 0:6144]
        hact_lo = PB[:, 6144:12288]
        wgu = [PB[:, 12288:14336], PB[:, 14336:16384]]
        wdt = [PB[:, 16384:18432], PB[:, 18432:20480]]
        wg_prod = [None, None]
        wg_cons = [[], []]
        wu_prod = [None, None]
        wu_cons = [[], []]
        wd_prod = [None, None]
        wd_cons = [[], []]
        sil_cons = [[], []]
        hfl_cons = [[], []]
        hact_cons = []
        xt2_prod = [None, None]
        xt2_cons = [[], []]
        outt_cons = [[], []]
        gt_cons = []
        for tb in range(8):
            tok0 = tb * 256
            hi_toks = []
            for hc in range(NHC):
                sl = hc % 2
                t1 = w.dma("sync",
                           wgu[sl][:, 0:1024].rearrange("p (dc j) -> p dc j", dc=8),
                           A(scr_wblk, hc * 2 * 8 * 16384,
                             [[128, 128], [16384, 8], [1, 128]]),
                           waits=wg_cons[sl], sem=ld_wg[sl])
                wg_prod[sl] = t1
                wg_cons[sl] = []
                t2 = w.dma("sync",
                           wgu[sl][:, 1024:2048].rearrange("p (dc j) -> p dc j", dc=8),
                           A(scr_wblk, (hc * 2 + 1) * 8 * 16384,
                             [[128, 128], [16384, 8], [1, 128]]),
                           waits=wu_cons[sl], sem=ld_wu[sl])
                wu_cons[sl] = []
                gbank = pb[sl]
                ubank = pb[2 + sl]
                gm = None
                um = None
                firstg = True
                firstu = True
                for xs in ("h", "l"):
                    for dc in range(8):
                        rhs = (xn_hi(dc) if xs == "h" else xn_lo(dc))[:, tok0:tok0 + 256]
                        gm = w.op("tensor",
                                  (lambda e, o=gbank[:, 0:256],
                                   l=wgu[sl][:, dc * 128:(dc + 1) * 128], r=rhs,
                                   st=firstg, sp=(xs == "l" and dc == 7):
                                   e.matmul(o, l, r, start=st, stop=sp)),
                                  waits=[t1] + (pbk_cons[sl] if firstg else []),
                                  inc=c_te)
                        firstg = False
                for xs in ("h", "l"):
                    for dc in range(8):
                        rhs = (xn_hi(dc) if xs == "h" else xn_lo(dc))[:, tok0:tok0 + 256]
                        um = w.op("tensor",
                                  (lambda e, o=ubank[:, 0:256],
                                   l=wgu[sl][:, 1024 + dc * 128:1024 + (dc + 1) * 128],
                                   r=rhs,
                                   st=firstu, sp=(xs == "l" and dc == 7):
                                   e.matmul(o, l, r, start=st, stop=sp)),
                                  waits=[t2] + (pbk_cons[2 + sl] if firstu else []),
                                  inc=c_te)
                        firstu = False
                wg_cons[sl] = [gm]
                wu_cons[sl] = [um]
                pbk_cons[sl] = []
                pbk_cons[2 + sl] = []
                s1 = w.op("scalar",
                          (lambda e, d=sil[sl], s_=gbank:
                           e.activation(d[:, :], s_[:, 0:256], AF.Silu,
                                        scale=1.0 / 127.0)),
                          waits=[gm] + sil_cons[sl], inc=c_se)
                pbk_cons[sl].append(s1)
                m1 = w.op("vector",
                          (lambda e, d=hfl[sl], a=sil[sl], s_=ubank:
                           e.scalar_tensor_tensor(d[:, :], s_[:, 0:256],
                                                  1.0 / 127.0, a[:, :],
                                                  AL.mult, AL.mult)),
                          waits=[s1, um] + hfl_cons[sl], inc=c_ve)
                pbk_cons[2 + sl].append(m1)
                sil_cons[sl] = [m1]
                h1 = w.op("scalar",
                          (lambda e, d=hact_hi[:, hc * 256:(hc + 1) * 256],
                           a=hfl[sl]:
                           e.activation(d, a[:, :], AF.Copy)),
                          waits=[m1] + hact_cons, inc=c_se)
                l1 = w.op("vector",
                          (lambda e, d=hact_lo[:, hc * 256:(hc + 1) * 256],
                           a=hfl[sl], h=hact_hi[:, hc * 256:(hc + 1) * 256]:
                           e.tensor_sub(d, a[:, :], h)),
                          waits=[h1], inc=c_ve)
                hfl_cons[sl] = [h1, l1]
                hi_toks.append(h1)
                hi_toks.append(l1)
            dm_last = [None] * 4
            for hc in range(NHC):
                sl = (hc // 2) % 2
                if hc % 2 == 0:
                    ta = w.dma("sync",
                               wdt[sl][:, 0:1024].rearrange("p (oc j) -> p oc j", oc=8),
                               A(scr_wblk, 2 * NHC * 8 * 16384 + hc * 8 * 16384,
                                 [[128, 128], [16384, 8], [1, 128]]),
                               waits=wd_cons[sl], sem=ld_wd[sl])
                    tb2 = w.dma("sync",
                                wdt[sl][:, 1024:2048].rearrange("p (oc j) -> p oc j", oc=8),
                                A(scr_wblk, 2 * NHC * 8 * 16384 + (hc + 1) * 8 * 16384,
                                  [[128, 128], [16384, 8], [1, 128]]),
                                waits=wd_cons[sl], sem=ld_wd[sl])
                    wd_prod[sl] = tb2
                    wd_cons[sl] = []
                t3 = wd_prod[sl]
                co = (hc % 2) * 1024
                dmt = None
                for tt2 in range(2):
                    for oh in range(2):
                        bank = pb[tt2 * 2 + oh]
                        for xs, HX in (("h", hact_hi), ("l", hact_lo)):
                            dmt = w.op("tensor",
                                       (lambda e, o=bank[:, 0:512],
                                        l=HX[:, hc * 256 + tt2 * 128:hc * 256 + tt2 * 128 + 128],
                                        r=wdt[sl][:, co + oh * 512:co + (oh + 1) * 512],
                                        st=(hc == 0 and xs == "h"),
                                        sp=(hc == NHC - 1 and xs == "l"):
                                        e.matmul(o, l, r, start=st, stop=sp)),
                                       waits=([t3, hi_toks[0], hi_toks[1]]
                                              + pbk_cons[tt2 * 2 + oh])
                                       if (hc == 0 and xs == "h") else [t3, hi_toks[2 * hc + (0 if xs == "h" else 1)]],
                                       inc=c_te)
                            dm_last[tt2 * 2 + oh] = dmt
                wd_cons[sl].append(dmt)
            for i in range(4):
                pbk_cons[i] = []
            for tt2 in range(2):
                tt = tb * 2 + tt2
                sg1 = w.op("scalar",
                           (lambda e, s_=pb[tt2 * 2 + 0]:
                            e.activation(gt[:, 0:512], s_[:, 0:512], AF.Silu,
                                         scale=1.0 / 127.0)),
                           waits=[dm_last[tt2 * 2 + 0]] + gt_cons, inc=c_se)
                sg2 = w.op("scalar",
                           (lambda e, s_=pb[tt2 * 2 + 1]:
                            e.activation(gt[:, 512:1024], s_[:, 0:512], AF.Silu,
                                         scale=1.0 / 127.0)),
                           waits=[dm_last[tt2 * 2 + 1]], inc=c_se)
                pbk_cons[tt2 * 2 + 0].append(sg1)
                pbk_cons[tt2 * 2 + 1].append(sg2)
                s = tt2 % 2
                xk = w.dma("sync", xt2[s][:, :],
                           A(x_in, (b * T + tt * 128) * D, [[D, 128], [1, D]]),
                           waits=xt2_cons[s], sem=ld_x2[s])
                xt2_prod[s] = xk
                xt2_cons[s] = []
                o1 = w.op("vector",
                          (lambda e, d=outt[s], tt=tt:
                           e.tensor_mul(d[:, :], s_tile(tt)[:, :], gt[:, :])),
                          waits=[sg1, sg2] + outt_cons[s] + (as_toks if tb == 0 else []),
                          inc=c_ve)
                gt_cons = [o1]
                o2 = w.op("vector",
                          (lambda e, d=outt[s], x_=xt2[s]:
                           e.tensor_add(d[:, :], d[:, :], x_[:, :])),
                          waits=[o1, xk], inc=c_ve)
                xt2_cons[s] = [o2]
                od = w.dma("scalar", A(y_out, (b * T + tt * 128) * D,
                                       [[D, 128], [1, D]]),
                           outt[s][:, :], waits=[o2], sem=st_y[s])
                outt_cons[s] = [od]
        w.fence()

    for e in ("scalar", "sync", "gpsimd"):
        toks = [Tok(s_, s_.count, e, True) for s_ in w.dma_sems[e] if s_.count]
        w.op(e, (lambda en: en.nop()), waits=toks)

    w.emit()
    ctx.close()
    return nc


_NC = None


def _get_nc():
    global _NC
    if _NC is None:
        _NC = build_program()
    return _NC


def kernel(L0_attn_x, L0_attn_Wq, L0_attn_bq, L0_attn_Wk, L0_attn_bk,
           L0_attn_Wv, L0_attn_bv, L0_attn_Wo, L0_attn_bo,
           L0_attn_gamma, L0_attn_beta,
           L0_ffn_w_gate, L0_ffn_w_up, L0_ffn_w_down):
    x = np.ascontiguousarray(np.asarray(L0_attn_x, dtype=np.float32))
    tri = np.triu(np.ones((128, 128), dtype=np.float32))
    maskbig = ((tri - 1.0) * 200.0).astype(ml_dtypes.bfloat16)
    identb = np.eye(128, dtype=ml_dtypes.bfloat16)
    identf = np.eye(128, dtype=np.float32)
    base = dict(
        wq=np.ascontiguousarray(np.asarray(L0_attn_Wq, dtype=np.float32)),
        wk=np.ascontiguousarray(np.asarray(L0_attn_Wk, dtype=np.float32)),
        wv=np.ascontiguousarray(np.asarray(L0_attn_Wv, dtype=np.float32)),
        wo=np.ascontiguousarray(np.asarray(L0_attn_Wo, dtype=np.float32)),
        bq=np.ascontiguousarray(np.asarray(L0_attn_bq, dtype=np.float32)),
        bk=np.ascontiguousarray(np.asarray(L0_attn_bk, dtype=np.float32)),
        bv=np.ascontiguousarray(np.asarray(L0_attn_bv, dtype=np.float32)),
        bo=np.ascontiguousarray(np.asarray(L0_attn_bo, dtype=np.float32)),
        gamma=np.ascontiguousarray(np.asarray(L0_attn_gamma, dtype=np.float32)),
        beta=np.ascontiguousarray(np.asarray(L0_attn_beta, dtype=np.float32)),
        wg=np.ascontiguousarray(np.asarray(L0_ffn_w_gate, dtype=np.int32)),
        wu=np.ascontiguousarray(np.asarray(L0_ffn_w_up, dtype=np.int32)),
        wd=np.ascontiguousarray(np.asarray(L0_ffn_w_down, dtype=np.int32)),
        tri=tri, maskbig=maskbig, identb=identb, identf=identf,
    )
    in_maps = []
    for c in range(8):
        m = dict(base)
        m["x"] = np.ascontiguousarray(x[2 * c:2 * c + 2])
        in_maps.append(m)
    nc = _get_nc()
    res = run_bass_kernel_spmd(nc, in_maps, core_ids=list(range(8)))
    out = np.concatenate([r["y"].reshape(2, T, D) for r in res.results], axis=0)
    return out.astype(np.float32)


# version 19 (final)
# speedup vs baseline: 1.0644x; 1.0644x over previous
import numpy as np
import ml_dtypes
from contextlib import ExitStack

import concourse.bass as bass
import concourse.mybir as mybir
from concourse.bass_utils import run_bass_kernel_spmd

F32 = mybir.dt.float32
BF16 = mybir.dt.bfloat16
I32 = mybir.dt.int32
AF = mybir.ActivationFunctionType
AL = mybir.AluOpType
AX = mybir.AxisListType

ENG = ("sync", "scalar", "vector", "tensor", "gpsimd")

B_PER_CORE = 2
T = 2048
D = 1024
H_LOC = 2
NDC = 8
HID = 3072
NHC = 24


class Sem:
    __slots__ = ("h", "count", "name")

    def __init__(self, h, name):
        self.h = h
        self.count = 0
        self.name = name


class Tok:
    __slots__ = ("sem", "val", "eng", "dma")

    def __init__(self, sem, val, eng, dma):
        self.sem = sem
        self.val = val
        self.eng = eng
        self.dma = dma


def _flat(waits):
    out = []
    for t in waits:
        if t is None:
            continue
        if isinstance(t, (list, tuple)):
            out.extend(_flat(t))
        else:
            out.append(t)
    return out


class W:
    def __init__(self, nc, ctx):
        self.nc = nc
        self.ctx = ctx
        self.ops = {e: [] for e in ENG}
        self.nsem = 0
        self.barrier = []
        self.fence_sem = self.sem("fence")
        self.qsem = {e: self.sem("q_" + e) for e in ("sync", "scalar", "gpsimd")}
        self.dma_sems = {e: [self.qsem[e]] for e in ("sync", "scalar", "gpsimd")}
        self.csem = {}
        self.phase_ops = {e: 0 for e in ENG}

    def sem(self, name):
        h = self.ctx.enter_context(self.nc.semaphore(name))
        return Sem(h, name)

    def dsem(self, name, q, n=1):
        out = []
        for i in range(n):
            s = self.sem("%s%d" % (name, i))
            self.dma_sems[q].append(s)
            out.append(s)
        return out if n > 1 else out[0]

    def op(self, eng, fn, waits=(), inc=None, dma=False):
        wl = {}
        for t in _flat(waits):
            k = id(t.sem)
            v = t.sem.count if t.dma else t.val
            if k not in wl or wl[k][1] < v:
                wl[k] = (t.sem, v)
        for t in self.barrier:
            k = id(t.sem)
            if k not in wl or wl[k][1] < t.val:
                wl[k] = (t.sem, t.val)
        n = 16 if dma else 1
        tok = None
        if inc is not None:
            inc.count += n
            tok = Tok(inc, inc.count, eng, dma)
        self.ops[eng].append((fn, list(wl.values()), inc, n))
        self.phase_ops[eng] += 1
        return tok

    def dma(self, q, out, in_, waits=(), slow=False, sem=None):
        return self.op(q, (lambda e, o=out, i=in_, s=slow:
                           e.dma_start(out=o, in_=i, allow_slow_non_contiguous=s)),
                       waits, inc=(sem or self.qsem[q]), dma=True)

    def fence(self):
        toks = []
        for e in ENG:
            if self.phase_ops[e] == 0:
                continue
            w = []
            if e in self.dma_sems:
                for q in self.dma_sems[e]:
                    if q.count:
                        w.append(Tok(q, q.count, e, True))
            cs = self.csem.get(e)
            if cs is not None and cs.count:
                w.append(Tok(cs, cs.count, e, True))
            toks.append(self.op(e, (lambda en: en.nop()), w, inc=self.fence_sem))
        self.barrier = toks
        for e in ENG:
            self.phase_ops[e] = 0

    def emit(self):
        nc = self.nc
        with nc.Block() as block:
            for e in ENG:
                def runner(eng, e=e):
                    mark = {}
                    for fn, waits, inc, n in self.ops[e]:
                        for sem, v in waits:
                            k = id(sem)
                            if mark.get(k, 0) >= v:
                                continue
                            eng.wait_ge(sem.h, v)
                            mark[k] = v
                        ins = fn(eng)
                        if inc is not None:
                            ins.then_inc(inc.h, n)
                getattr(block, e)(runner)


def A(t, off, dims):
    return bass.AP(t, off, [list(d) for d in dims])


def build_program():
    nc = bass.Bass("TRN2", target_bir_lowering=False)

    x_in = nc.dram_tensor("x", [B_PER_CORE, T, D], F32, kind="ExternalInput")
    wq_in = nc.dram_tensor("wq", [D, D], F32, kind="ExternalInput")
    wk_in = nc.dram_tensor("wk", [D, D], F32, kind="ExternalInput")
    wv_in = nc.dram_tensor("wv", [D, D], F32, kind="ExternalInput")
    wo_in = nc.dram_tensor("wo", [D, D], F32, kind="ExternalInput")
    bq_in = nc.dram_tensor("bq", [D], F32, kind="ExternalInput")
    bk_in = nc.dram_tensor("bk", [D], F32, kind="ExternalInput")
    bv_in = nc.dram_tensor("bv", [D], F32, kind="ExternalInput")
    bo_in = nc.dram_tensor("bo", [D], F32, kind="ExternalInput")
    gamma_in = nc.dram_tensor("gamma", [D], F32, kind="ExternalInput")
    beta_in = nc.dram_tensor("beta", [D], F32, kind="ExternalInput")
    wg_in = nc.dram_tensor("wg", [HID, D], I32, kind="ExternalInput")
    wu_in = nc.dram_tensor("wu", [HID, D], I32, kind="ExternalInput")
    wd_in = nc.dram_tensor("wd", [D, HID], I32, kind="ExternalInput")
    tri_in = nc.dram_tensor("tri", [128, 128], F32, kind="ExternalInput")
    mkb_in = nc.dram_tensor("maskbig", [128, 128], BF16, kind="ExternalInput")
    idb_in = nc.dram_tensor("identb", [128, 128], BF16, kind="ExternalInput")
    idf_in = nc.dram_tensor("identf", [128, 128], F32, kind="ExternalInput")
    y_out = nc.dram_tensor("y", [B_PER_CORE, T, D], F32, kind="ExternalOutput")

    scr_wblk = nc.dram_tensor("scr_wblk", [3, NHC, 8, 128, 128], BF16)
    scr_whilo = nc.dram_tensor("scr_whilo", [4, 2, D, D], BF16)
    scr_o = nc.dram_tensor("scr_o", [2, B_PER_CORE, D, T], BF16)

    ctx = ExitStack()
    SB = lambda n, s, d: ctx.enter_context(nc.sbuf_tensor(n, s, d))
    PS = lambda n, s, d: ctx.enter_context(nc.psum_tensor(n, s, d))

    AA = SB("arena_a", [128, 16 * T], BF16)
    AS = SB("arena_s", [128, 16 * 1024], F32)
    PF = SB("pool_f", [128, 8192], F32)
    PB = SB("pool_b", [128, 20480], BF16)
    tri_s = SB("tri_s", [128, 128], F32)
    mkb_s = SB("mkb_s", [128, 128], BF16)
    idb_s = SB("idb_s", [128, 128], BF16)
    idf_s = SB("idf_s", [128, 128], F32)
    bcols = SB("bcols", [128, 24], F32)
    tiny = SB("tiny", [128, 16], F32)
    trow = SB("trow", [1, 16], F32)
    ones_b = SB("ones_b", [128, 1], BF16)

    pb = [PS("pb%d" % i, [128, 512], F32) for i in range(6)]
    pbt2 = [PS("pbt", [128, 1024], BF16), PS("pbt2", [128, 1024], BF16)]
    pbt = pbt2[0]

    w = W(nc, ctx)
    ld_const = w.dsem("ld_const", "sync")
    ld_wt = w.dsem("ld_wt", "gpsimd", 3)
    st_stg = w.dsem("st_stg", "scalar")
    ld_wof = w.dsem("ld_wof", "gpsimd", 2)
    st_whl = w.dsem("st_whl", "scalar", 2)
    ld_x = w.dsem("ld_x", "sync", 2)
    ld_wsl = w.dsem("ld_wsl", "sync")
    st_osp = w.dsem("st_osp", "scalar")
    ld_wo = w.dsem("ld_wo", "sync")
    ld_ot = w.dsem("ld_ot", "sync", 4)
    ld_vec = w.dsem("ld_vec", "sync")
    ld_wg = w.dsem("ld_wg", "sync", 2)
    ld_wu = w.dsem("ld_wu", "sync", 2)
    ld_wd = w.dsem("ld_wd", "sync", 2)
    ld_x2 = w.dsem("ld_x2", "sync", 2)
    st_y = w.dsem("st_y", "scalar", 2)

    def xn_hi(dc):
        return AA[:, dc * T:(dc + 1) * T]

    def xn_lo(dc):
        return AA[:, (8 + dc) * T:(9 + dc) * T]

    def s_tile(tt):
        return AS[:, tt * 1024:(tt + 1) * 1024]

    EPS = tiny[:, 8:9]

    c_ve = w.sem("c_ve")
    c_se = w.sem("c_se")
    c_te = w.sem("c_te")
    c_ge = w.sem("c_ge")
    w.csem = {"vector": c_ve, "scalar": c_se, "tensor": c_te, "gpsimd": c_ge}

    ct = []
    ct.append(w.dma("sync", tri_s[:, :], tri_in[:, :], sem=ld_const))
    ct.append(w.dma("sync", mkb_s[:, :], mkb_in[:, :], sem=ld_const))
    ct.append(w.dma("sync", idb_s[:, :], idb_in[:, :], sem=ld_const))
    ct.append(w.dma("sync", idf_s[:, :], idf_in[:, :], sem=ld_const))
    for i, bt in enumerate((bq_in, bk_in, bv_in)):
        ct.append(w.dma("sync", bcols[:, i * 8:(i + 1) * 8],
                        A(bt, 0, [[1, 128], [128, 8]]), slow=True, sem=ld_const))
    ct.append(w.op("vector", lambda e: e.memset(tiny[:, 8:9], 1e-5), inc=c_ve))
    ct.append(w.op("vector", lambda e: e.memset(ones_b[:, :], 1.0), inc=c_ve))
    w.barrier = ct

    whl_all = []

    def emit_whilo():
        wof = [PF[:, 0:1024], PF[:, 1024:2048]]
        whib = [PB[:, 17408:18432], PB[:, 18432:19456]]
        wlob = [PB[:, 19456:20480], PB[:, 16384:17408]]
        wof_cons = [[], []]
        whl_cons = [[], []]
        for wi, wsrc in ((0, wq_in), (1, wk_in), (2, wv_in), (3, wo_in)):
            for dc in range(8):
                sl = dc % 2
                tok = w.dma("gpsimd", wof[sl][:, :],
                            A(wsrc, dc * 128 * D, [[D, 128], [1, D]]),
                            waits=wof_cons[sl], sem=ld_wof[sl])
                wof_cons[sl] = []
                hi = w.op("scalar",
                          (lambda e, d=whib[sl][:, :], s=wof[sl][:, :]:
                           e.activation(d, s, AF.Copy)),
                          waits=[tok] + whl_cons[sl], inc=c_se)
                lo = w.op("vector",
                          (lambda e, d=wlob[sl][:, :], s=wof[sl][:, :],
                           h=whib[sl][:, :]:
                           e.tensor_sub(d, s, h)),
                          waits=[tok, hi] + whl_cons[sl], inc=c_ve)
                wof_cons[sl] = [lo]
                t1 = w.dma("scalar",
                           A(scr_whilo, (wi * 2 + 0) * D * D + dc * 128 * D,
                             [[D, 128], [1, D]]), whib[sl][:, :], waits=[hi],
                           sem=st_whl[sl])
                t2 = w.dma("scalar",
                           A(scr_whilo, (wi * 2 + 1) * D * D + dc * 128 * D,
                             [[D, 128], [1, D]]), wlob[sl][:, :], waits=[lo],
                           sem=st_whl[sl])
                whl_cons[sl] = [t1, t2]
                whl_all.extend([t1, t2])

    wblk_state = {"units": None, "wtl_cons": None, "stg_cons": None,
                  "pbt_cons": None}

    def _wblk_unit(u):
        wtl = [PB[:, 17408:18432], PB[:, 18432:19456], PB[:, 19456:20480]]
        kind = u[0]
        sl = u[3]
        if kind == "gu":
            _, wi, hc, sl = u
            wsrc = wg_in if wi == 0 else wu_in
            tok = w.dma("gpsimd", wtl[sl][:, :],
                        A(wsrc, hc * 128 * D, [[D, 128], [1, D]]),
                        waits=[wblk_state["wtl_cons"][sl]], sem=ld_wt[sl])
            dst_off = (hc * 2 + wi) * 8 * 16384
            dims = [[128, 128], [16384, 8], [1, 128]]
        else:
            _, oc, g, sl = u
            tok = w.dma("gpsimd", wtl[sl][:, :],
                        A(wd_in, oc * 128 * HID + g * 1024,
                          [[HID, 128], [1, 1024]]),
                        waits=[wblk_state["wtl_cons"][sl]], sem=ld_wt[sl])
            dst_off = 2 * NHC * 8 * 16384 + g * 8 * 8 * 16384 + oc * 16384
            dims = [[128, 128], [8 * 16384, 8], [1, 128]]
        tt = None
        for dc in range(8):
            tt = w.op("tensor",
                      (lambda e, a=pbt2[1][:, dc * 128:(dc + 1) * 128],
                       b=wtl[sl][:, dc * 128:(dc + 1) * 128]:
                       e.transpose(a, b, idb_s[:, :])),
                      waits=[tok, wblk_state["pbt_cons"]] if dc == 0 else [tok],
                      inc=c_te)
        ev = w.op("vector",
                  (lambda e, d=wtl[sl][:, :]:
                   e.tensor_copy(d, pbt2[1][:, 0:1024])),
                  waits=[tt], inc=c_ve)
        wblk_state["pbt_cons"] = ev
        wblk_state["wtl_cons"][sl] = w.dma(
            "scalar", A(scr_wblk, dst_off, dims),
            wtl[sl][:, :].rearrange("p (j jj) -> p j jj", j=8),
            waits=[ev], sem=st_stg)

    def emit_wblk_units(n):
        st = wblk_state
        if st["units"] is None:
            units = []
            for wi in (0, 1):
                for hc in range(NHC):
                    units.append(("gu", wi, hc, (wi * NHC + hc) % 3))
            for oc in range(8):
                for g in range(3):
                    units.append(("wd", oc, g, (48 + oc * 3 + g) % 3))
            st["units"] = units
            st["wtl_cons"] = [whl_all, whl_all, whl_all]
            st["stg_cons"] = None
            st["pbt_cons"] = None
        k = 0
        while st["units"] and k < n:
            _wblk_unit(st["units"].pop(0))
            k += 1

    V2OFF = 3072
    pbk_cons = {i: [] for i in range(6)}
    lrow_cons = [[], []]
    wo_cons0 = []
    for b in range(B_PER_CORE):
        xt = [PF[:, 2048:3072], PF[:, 3072:4096]]
        sqt = [PF[:, 4096:5120], PF[:, 4096:5120]]
        xnt = [PF[:, 6144:7168], PF[:, 7168:8192]]
        su16 = PF[:, 5120:5136]
        ssq16 = PF[:, 5136:5152]
        negmu16 = PF[:, 5152:5168]
        m216 = PF[:, 5168:5184]
        r16 = PF[:, 5184:5200]
        xt_prod = [None, None]
        xt_cons = [[], []]
        xn_cons = [[], []]
        sq_cons = []
        pbtc = [[]]
        aa_done = {}
        st_toks = []
        for tt in range(16):
            s = tt % 2
            tok = w.dma("sync", xt[s][:, :],
                        A(x_in, (b * T + tt * 128) * D, [[D, 128], [1, D]]),
                        waits=xt_cons[s], sem=ld_x[s])
            xt_cons[s] = []
            sq = w.op("scalar", (lambda e, d=sqt[s], x_=xt[s], tt=tt:
                                 e.activation(d[:, :], x_[:, :], AF.Square,
                                              accum_out=ssq16[:, tt:tt + 1])),
                      waits=[tok] + sq_cons, inc=c_se)
            sq_cons = [sq]
            v1 = w.op("vector", (lambda e, x_=xt[s], tt=tt:
                                 e.reduce_sum(su16[:, tt:tt + 1], x_[:, :],
                                              axis=AX.X)),
                      waits=[tok], inc=c_ve)
            xt_cons[s] = [sq, v1]
            st_toks += [sq, v1]
        v3 = w.op("vector", (lambda e: e.tensor_scalar_mul(negmu16, su16,
                                                           -1.0 / 1024.0)),
                  waits=st_toks, inc=c_ve)
        v4 = w.op("vector", (lambda e: e.tensor_mul(m216, negmu16, negmu16)),
                  waits=[v3], inc=c_ve)
        v5 = w.op("vector", (lambda e: e.scalar_tensor_tensor(
            m216, ssq16, 1.0 / 1024.0, m216, AL.mult, AL.subtract)),
                  waits=[v4] + st_toks, inc=c_ve)
        v5e = w.op("vector", (lambda e: e.tensor_scalar_add(m216, m216, 1e-5)),
                   waits=[v5], inc=c_ve)
        s1 = w.op("scalar", (lambda e: e.activation(r16, m216, AF.Ln)),
                  waits=[v5e], inc=c_se)
        s2 = w.op("scalar", (lambda e: e.activation(r16, r16, AF.Exp, scale=-0.5)),
                  waits=[s1], inc=c_se)
        rb16 = PF[:, 5200:5216]
        s3 = w.op("vector", (lambda e: e.tensor_mul(rb16, negmu16, r16)),
                  waits=[s2], inc=c_ve)
        apend = []

        def flush_atr(item):
            s, tt, v6 = item
            for g2_ in range(2):
                bank = pb[(tt * 2 + g2_) % 6]
                tks = []
                for j in range(4):
                    dc = g2_ * 4 + j
                    tks.append(w.op("tensor",
                                    (lambda e, o=bank[:, j * 128:(j + 1) * 128],
                                     s_=xnt[s][:, dc * 128:(dc + 1) * 128]:
                                     e.transpose(o, s_, idf_s[:, :])),
                                    waits=[v6] + pbk_cons[(tt * 2 + g2_) % 6],
                                    inc=c_te))
                xn_cons[s].append(tks[-1])
                pbk_cons[(tt * 2 + g2_) % 6] = []
                aa3 = AA[:, :].rearrange("p (dc t) -> p dc t", dc=16)
                hi = w.op("scalar",
                          (lambda e, tt=tt, g2_=g2_, bank=bank,
                           d=aa3[:, g2_ * 4:g2_ * 4 + 4, tt * 128:(tt + 1) * 128]:
                           e.activation(d,
                                        bank[:, 0:512].rearrange(
                                            "p (dc j) -> p dc j", dc=4),
                                        AF.Copy)),
                          waits=[tks[-1]], inc=c_se)
                lo = w.op("vector",
                          (lambda e, tt=tt, g2_=g2_, bank=bank,
                           d=aa3[:, 8 + g2_ * 4:8 + g2_ * 4 + 4,
                                 tt * 128:(tt + 1) * 128],
                           h=aa3[:, g2_ * 4:g2_ * 4 + 4, tt * 128:(tt + 1) * 128]:
                           e.tensor_sub(d,
                                        bank[:, 0:512].rearrange(
                                            "p (dc j) -> p dc j", dc=4), h)),
                          waits=[tks[-1], hi], inc=c_ve)
                pbk_cons[(tt * 2 + g2_) % 6] = [hi, lo]
                aa_done.setdefault(tt // 4, []).append(lo)

        for tt in range(16):
            s = tt % 2
            tok = w.dma("sync", xt[s][:, :],
                        A(x_in, (b * T + tt * 128) * D, [[D, 128], [1, D]]),
                        waits=xt_cons[s], sem=ld_x[s])
            xt_cons[s] = []
            v6 = w.op("scalar", (lambda e, d=xnt[s], x_=xt[s], tt=tt:
                                 e.activation(d[:, :], x_[:, :], AF.Identity,
                                              bias=rb16[:, tt:tt + 1],
                                              scale=r16[:, tt:tt + 1])),
                      waits=[tok, s3] + xn_cons[s], inc=c_se)
            xt_cons[s] = [v6]
            xn_cons[s] = []
            apend.append((s, tt, v6))
            if len(apend) > 1:
                flush_atr(apend.pop(0))
        while apend:
            flush_atr(apend.pop(0))
        if b == 0:
            emit_whilo()
        w.fence()

        kqev = [PF[:, 0:512], PF[:, 512:1024]]
        vfm = PF[:, 1024:3072]
        linvb = vfm[:, 0:512]
        V2 = PF[:, V2OFF:V2OFF + 2080]
        attb = [[PF[:, 5152:5664], PF[:, 5664:6176]],
                [PF[:, 6176:6688], PF[:, 6688:7200]]]
        ofst = PF[:, 7200:7712]
        KHI = PB[:, 0:2048]
        KLO = PB[:, 2048:4096]
        QHI = PB[:, 4096:6144]
        QLO = PB[:, 6144:8192]
        WSL = {}
        for i, nm in enumerate(("wqh", "wql", "wkh", "wkl", "wvh", "wvl")):
            WSL[nm] = PB[:, 8192 + i * 1024:8192 + (i + 1) * 1024]
        ksqb = [PB[:, 14336:14848], PB[:, 14848:15360]]
        ohst = [PB[:, 15360 + 512 * i:15360 + 512 * (i + 1)] for i in range(4)]

        NEGM = tiny[:, 0:2]
        LV = tiny[:, 2:4]

        v2cons = []
        m1 = w.op("vector",
                  (lambda e: e.memset(
                      V2.rearrange("p (tt g j) -> p tt g j", g=2, j=65)[:, :, :, 64:65],
                      1.0)), inc=c_ve)
        v2ones = m1

        wsl_prod = {}
        wsl_cons = {}
        pbk_cons = {i: [] for i in range(6)}
        ksq_cons2 = [[]]
        trow_cons2 = [[]]
        evac_pend = [None]
        ofst_cons2 = [[]]
        linvb_cons2 = [[]]
        vfm_cons2 = [[]]
        v2_cons2 = [[]]
        kq_cons = [[], []]
        khql_cons = []
        ksq_cons = [[], []]
        att_cons = [[[], []], [[], []]]
        oh_cons = [[], [], [], []]
        negm_cons = []
        trow_cons = []

        for hg in range(8):
            rd_prev = {}
            for i, (wi, hl, nm) in enumerate(
                    ((0, 0, "wqh"), (0, 1, "wql"), (1, 0, "wkh"),
                     (1, 1, "wkl"), (2, 0, "wvh"), (2, 1, "wvl"))):
                tok = w.dma("sync", WSL[nm][:, :].rearrange("p (dc j) -> p dc j", dc=8),
                            A(scr_whilo, (wi * 2 + hl) * D * D + hg * 128,
                              [[D, 128], [128 * D, 8], [1, 128]]),
                            waits=wsl_cons.get(nm, []) + whl_all, sem=ld_wsl)
                wsl_prod[nm] = tok
                wsl_cons[nm] = []

            mhat_pend = []

            def flush_mhat(item):
                sq, tc, mcol = item
                bq_toks = []
                for hh in range(2):
                    bnk = pb[4 + hh]
                    mt = w.op("tensor",
                              (lambda e, o=bnk[0:1, 0:512],
                               l=ones_b[hh * 64:(hh + 1) * 64, 0:1],
                               r=ksqb[tc % 2][hh * 64:(hh + 1) * 64, :],
                               tp=(hh * 64, 0):
                               e.matmul(o, l, r, start=True, stop=True,
                                        tile_position=tp)),
                              waits=[sq] + pbk_cons[4 + hh], inc=c_te)
                    pbk_cons[4 + hh] = []
                    if (mcol, hh) not in rd_prev:
                        rd = w.op("vector",
                                  (lambda e, d=trow[0:1, mcol + hh:mcol + hh + 1],
                                   s_=bnk[0:1, 0:512]:
                                   e.reduce_max(d, s_, axis=AX.X)),
                                  waits=[mt] + trow_cons2[0], inc=c_ve)
                    else:
                        tc_ = 4 + hh + (0 if mcol == 2 else 2)
                        r1 = w.op("vector",
                                  (lambda e, s_=bnk[0:1, 0:512], c=tc_:
                                   e.reduce_max(trow[0:1, c:c + 1], s_,
                                                axis=AX.X)),
                                  waits=[mt, rd_prev.get((mcol, hh))], inc=c_ve)
                        rd = w.op("vector",
                                  (lambda e, d=trow[0:1, mcol + hh:mcol + hh + 1],
                                   c=tc_:
                                   e.tensor_max(d, d, trow[0:1, c:c + 1])),
                                  waits=[r1], inc=c_ve)
                    rd_prev[(mcol, hh)] = rd
                    pbk_cons[4 + hh].append(rd)
                    bq_toks.append(rd)
                ksq_cons2[0] = bq_toks
                trow_cons2[0] = bq_toks

            for which, (whn, wln, DHI, DLO, bcol0, scl, mcol) in enumerate(
                    (("wkh", "wkl", KHI, KLO, 8, 0.5, 2),
                     ("wqh", "wql", QHI, QLO, 0, 0.25, 0))):
                for tc in range(4):
                    bank = pb[tc % 2]
                    mmt = None
                    first = True
                    for (wn, xs) in ((whn, "h"), (whn, "l"), (wln, "h")):
                        wt = WSL[wn]
                        for dc in range(8):
                            rhs = (xn_hi(dc) if xs == "h" else xn_lo(dc))[:, tc * 512:(tc + 1) * 512]
                            mmt = w.op("tensor",
                                       (lambda e, o=bank[:, 0:512],
                                        l=wt[:, dc * 128:(dc + 1) * 128], r=rhs,
                                        st=first, sp=(wn == wln and dc == 7):
                                        e.matmul(o, l, r, start=st, stop=sp)),
                                       waits=([wsl_prod[wn]] + pbk_cons[tc % 2]
                                              + aa_done.get(tc, []) + khql_cons)
                                       if first else [wsl_prod[wn]],
                                       inc=c_te)
                            first = False
                    wsl_cons.setdefault(whn, []).append(mmt)
                    wsl_cons.setdefault(wln, []).append(mmt)
                    pbk_cons[tc % 2] = []
                    ev = w.op("vector",
                              (lambda e, d=kqev[tc % 2], s_=bank,
                               bc=bcols[:, bcol0 + hg:bcol0 + hg + 1], sc=scl:
                               e.tensor_scalar(d[:, :], s_[:, 0:512], bc, sc,
                                               AL.add, AL.mult)),
                              waits=[mmt] + kq_cons[tc % 2], inc=c_ve)
                    pbk_cons[tc % 2].append(ev)
                    kq_cons[tc % 2] = []
                    sq = w.op("scalar",
                              (lambda e, d=ksqb[tc % 2], s_=kqev[tc % 2]:
                               e.activation(d[:, :], s_[:, :], AF.Square)),
                              waits=[ev] + ksq_cons2[0], inc=c_se)
                    kq_cons[tc % 2].append(sq)
                    hi = w.op("scalar",
                              (lambda e, d=DHI[:, tc * 512:(tc + 1) * 512],
                               s_=kqev[tc % 2]:
                               e.activation(d, s_[:, :], AF.Copy)),
                              waits=[ev], inc=c_se)
                    lo = w.op("vector",
                              (lambda e, d=DLO[:, tc * 512:(tc + 1) * 512],
                               s_=kqev[tc % 2], h=DHI[:, tc * 512:(tc + 1) * 512]:
                               e.tensor_sub(d, s_[:, :], h)),
                              waits=[hi], inc=c_ve)
                    kq_cons[tc % 2].extend([hi, lo])
                    mhat_pend.append((sq, tc, mcol))
                    if len(mhat_pend) > 1:
                        flush_mhat(mhat_pend.pop(0))

            while mhat_pend:
                flush_mhat(mhat_pend.pop(0))

            vpend = []

            def flush_vtr(item):
                tc, ev = item
                bankt = pb[2 + tc % 2]
                tks = []
                for j in range(4):
                    tks.append(w.op("tensor",
                                    (lambda e, o=bankt[:, j * 128:(j + 1) * 128],
                                     s_=vfm[:, (tc * 4 + j) * 128:(tc * 4 + j + 1) * 128]:
                                     e.transpose(o, s_, idf_s[:, :])),
                                    waits=[ev] + pbk_cons[2 + tc % 2], inc=c_te))
                pbk_cons[2 + tc % 2] = []
                vfm_cons2[0].append(tks[-1])
                evv = w.op("vector",
                           (lambda e, tc=tc, s_=bankt:
                            e.tensor_copy(
                                V2.rearrange("p (tt g j) -> p tt g j", g=2, j=65)
                                [:, tc * 4:(tc + 1) * 4, :, 0:64],
                                s_[:, 0:512].rearrange("p (tt g j) -> p tt g j",
                                                       g=2, j=64))),
                           waits=[tks[-1], v2ones] + v2_cons2[0], inc=c_ve)
                v2_cons2[0] = []
                pbk_cons[2 + tc % 2].append(evv)
                v2cons.append(evv)

            for tc in range(4):
                bank = pb[tc % 2]
                mmt = None
                first = True
                for (wn, xs) in (("wvh", "h"), ("wvh", "l"), ("wvl", "h")):
                    wt = WSL[wn]
                    for dc in range(8):
                        rhs = (xn_hi(dc) if xs == "h" else xn_lo(dc))[:, tc * 512:(tc + 1) * 512]
                        mmt = w.op("tensor",
                                   (lambda e, o=bank[:, 0:512],
                                    l=wt[:, dc * 128:(dc + 1) * 128], r=rhs,
                                    st=first, sp=(wn == "wvl" and dc == 7):
                                    e.matmul(o, l, r, start=st, stop=sp)),
                                   waits=[wsl_prod[wn]] + (pbk_cons[tc % 2] if first else []),
                                   inc=c_te)
                        first = False
                wsl_cons.setdefault("wvh", []).append(mmt)
                wsl_cons.setdefault("wvl", []).append(mmt)
                pbk_cons[tc % 2] = []
                ev = w.op("vector",
                          (lambda e, d=vfm[:, tc * 512:(tc + 1) * 512], s_=bank,
                           bc=bcols[:, 16 + hg:17 + hg]:
                           e.tensor_scalar_add(d, s_[:, 0:512], bc)),
                          waits=[mmt] + vfm_cons2[0], inc=c_ve)
                vfm_cons2[0] = []
                pbk_cons[tc % 2].append(ev)
                vpend.append((tc, ev))
                if len(vpend) > 1:
                    flush_vtr(vpend.pop(0))
            while vpend:
                flush_vtr(vpend.pop(0))

            g1 = w.op("vector",
                      (lambda e: e.tensor_add(trow[0:1, 8:10], trow[0:1, 0:2],
                                              trow[0:1, 2:4])),
                      waits=trow_cons2[0] + ksq_cons2[0]
                      + list(rd_prev.values()), inc=c_ve)
            g2 = w.op("vector",
                      (lambda e: e.tensor_scalar(trow[0:1, 8:10], trow[0:1, 8:10],
                                                 -0.505, -0.01, AL.mult, AL.add)),
                      waits=[g1], inc=c_ve)
            g3 = w.op("tensor",
                      (lambda e: e.matmul(pb[5][:, 0:2], tri_s[0:1, 0:128],
                                          trow[0:1, 8:10], start=True, stop=True)),
                      waits=[g2] + pbk_cons[5], inc=c_te)
            g4 = w.op("vector",
                      (lambda e: e.tensor_copy(NEGM, pb[5][:, 0:2])),
                      waits=[g3] + negm_cons, inc=c_ve)
            pbk_cons[5] = [g4]
            negm_tok = g4
            trow_cons2[0] = [g1]
            negm_cons = []

            def emit_evac(av_l, sh, qb_e, rcs):
                mul_toks = []
                for hh in range(2):
                    bank = pb[3 + hh]
                    lrow = vfm[0:1, 1024 + hh * 512:1024 + hh * 512 + 512]
                    rc = rcs[hh]
                    bm = w.op("tensor",
                              (lambda e, d=lrow:
                               e.matmul(pb[5][:, 0:512], tri_s[0:1, 0:128], d,
                                        start=True, stop=True)),
                              waits=[rc] + pbk_cons[5], inc=c_te)
                    cp = w.op("scalar",
                              (lambda e: e.copy(linvb[:, :], pb[5][:, 0:512])),
                              waits=[bm] + linvb_cons2[0], inc=c_se)
                    pbk_cons[5] = [cp]
                    ml = w.op("vector",
                              (lambda e, hh=hh, s_=bank:
                               e.tensor_mul(ofst[hh * 64:(hh + 1) * 64, 0:512],
                                            s_[0:64, 0:512],
                                            linvb[0:64, 0:512])),
                              waits=[cp] + ofst_cons2[0], inc=c_ve)
                    linvb_cons2[0] = [ml]
                    lrow_cons[hh] = [bm]
                    pbk_cons[3 + hh] = [ml]
                    mul_toks.append(ml)
                ofst_cons2[0] = []
                hi = w.op("scalar",
                          (lambda e, d=ohst[sh]: e.activation(d[:, :], ofst[:, :],
                                                              AF.Copy)),
                          waits=mul_toks + oh_cons[sh], inc=c_se)
                lo = w.op("vector",
                          (lambda e, d=ohst[2 + sh], h=ohst[sh]:
                           e.tensor_sub(d[:, :], ofst[:, :], h[:, :])),
                          waits=[hi] + oh_cons[2 + sh], inc=c_ve)
                ofst_cons2[0] = [lo]
                d1 = w.dma("scalar",
                           A(scr_o, (b * D + hg * 128) * T + qb_e * 512,
                             [[T, 128], [1, 512]]),
                           ohst[sh][:, :], waits=[hi], sem=st_osp)
                d2 = w.dma("scalar",
                           A(scr_o, ((B_PER_CORE + b) * D + hg * 128) * T + qb_e * 512,
                             [[T, 128], [1, 512]]),
                           ohst[2 + sh][:, :], waits=[lo], sem=st_osp)
                oh_cons[sh] = [d1]
                oh_cons[2 + sh] = [d2]

            for qb in range(4):
                nsc = 4 * qb + 4
                exp_toks = {}
                mask_toks = {}
                av_last = {}

                def emit_scores(sc):
                    s0 = sc * 128
                    qlo = max(qb * 512, s0)
                    N = qb * 512 + 512 - qlo
                    toks = {}
                    for hh in range(2):
                        bank = pb[(2 * sc + hh) % 3]
                        diag = s0 >= qb * 512
                        mmt = None
                        first = True
                        for KX, QX in ((KHI, QHI), (KHI, QLO), (KLO, QHI)):
                            mmt = w.op("tensor",
                                       (lambda e, o=bank[:, 0:N],
                                        l=KX[hh * 64:(hh + 1) * 64, s0:s0 + 128],
                                        r=QX[hh * 64:(hh + 1) * 64, qlo:qlo + N],
                                        st=first, sp=(KX is KLO and not diag),
                                        tp=(hh * 64, 0):
                                        e.matmul(o, l, r, start=st, stop=sp,
                                                 tile_position=tp)),
                                       waits=(pbk_cons[(2 * sc + hh) % 3]
                                              + khql_cons_local) if first else [],
                                       inc=c_te)
                            first = False
                        if diag:
                            mmt = w.op("tensor",
                                       (lambda e, o=bank[:, 0:128]:
                                        e.matmul(o, idb_s[:, :], mkb_s[:, :],
                                                 start=False, stop=True)),
                                       inc=c_te)
                        pbk_cons[(2 * sc + hh) % 3] = []
                        toks[hh] = mmt
                    return toks, qlo, N

                def emit_exp(sc, toks, qlo, N):
                    s0 = sc * 128
                    slot = sc % 2
                    for hh in range(2):
                        bank = pb[(2 * sc + hh) % 3]
                        ex = w.op("scalar",
                                  (lambda e, d=attb[slot][hh][:, 0:N],
                                   s_=bank[:, 0:N], bc=NEGM[:, hh:hh + 1]:
                                   e.activation(d, s_, AF.Exp, bias=bc)),
                                  waits=[toks[hh], negm_tok] + att_cons[slot][hh],
                                  inc=c_se)
                        att_cons[slot][hh] = []
                        pbk_cons[(2 * sc + hh) % 3].append(ex)
                        exp_toks[(sc, hh)] = ex

                def emit_av(sc):
                    s0 = sc * 128
                    qlo = max(qb * 512, s0)
                    N = qb * 512 + 512 - qlo
                    qoff = qlo - qb * 512
                    slot = sc % 2
                    for hh in range(2):
                        bank = pb[3 + hh]
                        wts = [exp_toks[(sc, hh)]]
                        if sc == 0:
                            wts += pbk_cons[3 + hh] + v2cons_w
                        mmt = w.op("tensor",
                                   (lambda e, o=bank[0:65, qoff:qoff + N],
                                    l=V2[:, sc * 130 + hh * 65:sc * 130 + hh * 65 + 65],
                                    r=attb[slot][hh][:, 0:N],
                                    st=(sc == 0), sp=(sc == nsc - 1):
                                    e.matmul(o, l, r, start=st, stop=sp)),
                                   waits=wts, inc=c_te)
                        att_cons[slot][hh].append(mmt)
                        av_last[hh] = mmt
                    if sc == 0:
                        pbk_cons[3] = []
                        pbk_cons[4] = []

                khql_cons_local = kq_cons[0] + kq_cons[1]
                v2cons_w = list(v2cons)
                prev = None
                for sc in range(nsc):
                    tks, qlo, N = emit_scores(sc)
                    emit_exp(sc, tks, qlo, N)
                    if sc == 0 and evac_pend[0] is not None:
                        emit_evac(*evac_pend[0])
                        evac_pend[0] = None
                    if prev is not None:
                        emit_av(prev)
                    prev = sc
                emit_av(prev)
                if b == 0:
                    emit_wblk_units(2)
                rcs = {}
                for hh in range(2):
                    lrow = vfm[0:1, 1024 + hh * 512:1024 + hh * 512 + 512]
                    rc0 = w.op("scalar",
                               (lambda e, d=lrow, s_=pb[3 + hh]:
                                e.activation(d, s_[64:65, 0:512], AF.Ln)),
                               waits=[av_last[hh]] + lrow_cons[hh], inc=c_se)
                    rcs[hh] = w.op("scalar",
                                   (lambda e, d=lrow:
                                    e.activation(d, d, AF.Exp, scale=-1.0)),
                                   waits=[rc0], inc=c_se)
                evac_pend[0] = (dict(av_last), qb % 2, qb, rcs)

            if evac_pend[0] is not None:
                emit_evac(*evac_pend[0])
                evac_pend[0] = None
            if b == 0 and hg == 7:
                emit_wblk_units(72)

            khql_cons = [t for hh in range(2) for sl2 in range(2)
                         for t in att_cons[sl2][hh]]
            v2_cons2[0] = [t for hh in range(2) for sl2 in range(2)
                           for t in att_cons[sl2][hh]]
            v2cons = []
            vfm_cons2[0] = vfm_cons2[0] + lrow_cons[0] + lrow_cons[1] + linvb_cons2[0]
            negm_cons = [exp_toks[k] for k in exp_toks][-1:]
        w.fence()

        gamb = PF[:, 0:1024]
        betb = PF[:, 1024:2048]
        bob = PF[:, 2048:3072]
        sqc = PF[:, 3072:4096]
        statc16 = PF[:, 4096:4176]
        ofm_hi = [PB[:, 0:1024], PB[:, 1024:2048]]
        ofm_lo = [PB[:, 2048:3072], PB[:, 3072:4096]]
        woh = PB[:, 4096:12288]
        wol = PB[:, 12288:20480]

        bc_toks = []
        ld_toks = []
        for vin, dst in ((gamma_in, gamb), (beta_in, betb), (bo_in, bob)):
            ld_toks.append(w.dma("sync", dst[0:1, :],
                                 A(vin, 0, [[1024, 1], [1, 1024]]), sem=ld_vec))
        for vi, (vin, dst) in enumerate(((gamma_in, gamb), (beta_in, betb),
                                         (bo_in, bob))):
            cts = []
            for half in range(2):
                bk = pb[4 + half]
                mmb = w.op("tensor",
                           (lambda e, d=dst, hf=half, b_=bk:
                            e.matmul(b_[:, 0:512], tri_s[0:1, 0:128],
                                     d[0:1, hf * 512:(hf + 1) * 512],
                                     start=True, stop=True)),
                           waits=ld_toks + pbk_cons[4 + half], inc=c_te)
                cp = w.op("vector",
                          (lambda e, d=dst, hf=half, b_=bk:
                           e.tensor_copy(d[:, hf * 512:(hf + 1) * 512],
                                         b_[:, 0:512])),
                          waits=[mmb], inc=c_ve)
                pbk_cons[4 + half] = [cp]
                cts.append(cp)
            bc_toks.append(cts)
        w1 = w.dma("sync", woh[:, :].rearrange("p (dc j) -> p dc j", dc=8),
                   A(scr_whilo, 6 * D * D, [[D, 128], [128 * D, 8], [1, D]]),
                   waits=wo_cons0, sem=ld_wo)
        w2 = w.dma("sync", wol[:, :].rearrange("p (dc j) -> p dc j", dc=8),
                   A(scr_whilo, 7 * D * D, [[D, 128], [128 * D, 8], [1, D]]),
                   waits=wo_cons0, sem=ld_wo)
        ofm_cons = []
        y_cons = {}
        as_toks = []
        ofm_slot_cons = [[], []]
        for tt in range(16):
            sl8 = tt % 2
            l1 = w.dma("sync",
                       ofm_hi[sl8][:, :].rearrange("p (dc j) -> p dc j", dc=8),
                       A(scr_o, (b * D) * T + tt * 128,
                         [[T, 128], [128 * T, 8], [1, 128]]),
                       waits=ofm_slot_cons[sl8], sem=ld_ot[sl8])
            l2 = w.dma("sync",
                       ofm_lo[sl8][:, :].rearrange("p (dc j) -> p dc j", dc=8),
                       A(scr_o, ((B_PER_CORE + b) * D) * T + tt * 128,
                         [[T, 128], [128 * T, 8], [1, 128]]),
                       waits=ofm_slot_cons[sl8], sem=ld_ot[2 + sl8])
            ofm_slot_cons[sl8] = []
            for oh in range(2):
                bank = pb[(tt * 2 + oh) % 4]
                mmt = None
                first = True
                for OX, WX, ltok in (
                        (ofm_hi[sl8], woh, l1), (ofm_hi[sl8], wol, l1),
                        (ofm_lo[sl8], woh, l2)):
                    lo_last = OX is ofm_lo[sl8]
                    wo_lo = WX is wol
                    for dc in range(8):
                        mmt = w.op("tensor",
                                   (lambda e, o=bank[:, 0:512],
                                    l=OX[:, dc * 128:(dc + 1) * 128],
                                    r=WX[:, dc * 1024 + oh * 512:dc * 1024 + (oh + 1) * 512],
                                    st=first,
                                    sp=(lo_last and dc == 7):
                                    e.matmul(o, l, r, start=st, stop=sp)),
                                   waits=([l1, l2, w1, w2] + pbk_cons[(tt * 2 + oh) % 4])
                                   if first else [w1 if not wo_lo else w2, ltok],
                                   inc=c_te)
                        first = False
                pbk_cons[(tt * 2 + oh) % 4] = []
                ofm_slot_cons[sl8].append(mmt)
                ev = w.op("vector",
                          (lambda e, tt=tt, oh=oh, s_=bank:
                           e.tensor_add(s_tile(tt)[:, oh * 512:(oh + 1) * 512],
                                        s_[:, 0:512],
                                        bob[:, oh * 512:(oh + 1) * 512])),
                          waits=[mmt, bc_toks[2]] + y_cons.get(tt, []),
                          inc=c_ve)
                pbk_cons[(tt * 2 + oh) % 4].append(ev)
                y_cons.setdefault(tt, []).append(ev)
        wo_cons0 = [t for ts_ in ofm_slot_cons for t in ts_]
        suc = statc16[:, 0:16]
        ssqc = statc16[:, 16:32]
        negmuc = statc16[:, 32:48]
        m2c = statc16[:, 48:64]
        rc16 = statc16[:, 64:80]
        sq_cons = []
        stc_toks = []
        for tt in range(16):
            yt = s_tile(tt)
            sq = w.op("scalar", (lambda e, a=yt, tt=tt:
                                 e.activation(sqc[:, :], a[:, :], AF.Square,
                                              accum_out=ssqc[:, tt:tt + 1])),
                      waits=y_cons.get(tt, []) + sq_cons, inc=c_se)
            sq_cons = [sq]
            v1 = w.op("vector", (lambda e, a=yt, tt=tt:
                                 e.reduce_sum(suc[:, tt:tt + 1], a[:, :],
                                              axis=AX.X)),
                      waits=y_cons.get(tt, []), inc=c_ve)
            stc_toks += [sq, v1]
        v3 = w.op("vector", (lambda e: e.tensor_scalar_mul(negmuc, suc,
                                                           -1.0 / 1024.0)),
                  waits=stc_toks, inc=c_ve)
        v4 = w.op("vector", (lambda e: e.tensor_mul(m2c, negmuc, negmuc)),
                  waits=[v3], inc=c_ve)
        v5 = w.op("vector", (lambda e: e.scalar_tensor_tensor(
            m2c, ssqc, 1.0 / 1024.0, m2c, AL.mult, AL.subtract)),
                  waits=[v4] + stc_toks, inc=c_ve)
        v5e = w.op("vector", (lambda e: e.tensor_scalar_add(m2c, m2c, 1e-5)),
                   waits=[v5], inc=c_ve)
        s1 = w.op("scalar", (lambda e: e.activation(rc16, m2c, AF.Ln)),
                  waits=[v5e], inc=c_se)
        s2 = w.op("scalar", (lambda e: e.activation(rc16, rc16, AF.Exp,
                                                    scale=-0.5)),
                  waits=[s1], inc=c_se)
        for tt in range(16):
            yt = s_tile(tt)
            v6 = w.op("vector", (lambda e, a=yt, tt=tt:
                                 e.tensor_scalar(a[:, :], a[:, :],
                                                 negmuc[:, tt:tt + 1],
                                                 rc16[:, tt:tt + 1],
                                                 AL.add, AL.mult)),
                      waits=[s2], inc=c_ve)
            v7 = w.op("vector", (lambda e, a=yt:
                                 e.tensor_mul(a[:, :], a[:, :], gamb[:, :])),
                      waits=[v6, bc_toks[0]], inc=c_ve)
            v8 = w.op("vector", (lambda e, a=yt:
                                 e.tensor_add(a[:, :], a[:, :], betb[:, :])),
                      waits=[v7, bc_toks[1]], inc=c_ve)
            as_toks.append(v8)
        w.fence()

        sil = [PF[:, 0:256], PF[:, 256:512]]
        hfl = [PF[:, 512:768], PF[:, 768:1024]]
        gt = PF[:, 1024:2048]
        xt2 = [PF[:, 2048:3072], PF[:, 3072:4096]]
        outt = [PF[:, 4096:5120], PF[:, 5120:6144]]
        hact_hi = PB[:, 0:6144]
        hact_lo = PB[:, 6144:12288]
        wgu = [PB[:, 12288:14336], PB[:, 14336:16384]]
        wdt = [PB[:, 16384:18432], PB[:, 18432:20480]]
        wg_prod = [None, None]
        wg_cons = [[], []]
        wu_prod = [None, None]
        wu_cons = [[], []]
        wd_prod = [None, None]
        wd_cons = [[], []]
        sil_cons = [[], []]
        hfl_cons = [[], []]
        hact_cons = []
        xt2_prod = [None, None]
        xt2_cons = [[], []]
        outt_cons = [[], []]
        gt_cons = []
        for tb in range(8):
            tok0 = tb * 256
            hi_toks = []
            for hc in range(NHC):
                sl = hc % 2
                t1 = w.dma("sync",
                           wgu[sl][:, 0:1024].rearrange("p (dc j) -> p dc j", dc=8),
                           A(scr_wblk, hc * 2 * 8 * 16384,
                             [[128, 128], [16384, 8], [1, 128]]),
                           waits=wg_cons[sl], sem=ld_wg[sl])
                wg_prod[sl] = t1
                wg_cons[sl] = []
                t2 = w.dma("sync",
                           wgu[sl][:, 1024:2048].rearrange("p (dc j) -> p dc j", dc=8),
                           A(scr_wblk, (hc * 2 + 1) * 8 * 16384,
                             [[128, 128], [16384, 8], [1, 128]]),
                           waits=wu_cons[sl], sem=ld_wu[sl])
                wu_cons[sl] = []
                gbank = pb[sl]
                ubank = pb[2 + sl]
                gm = None
                um = None
                firstg = True
                firstu = True
                for xs in ("h", "l"):
                    for dc in range(8):
                        rhs = (xn_hi(dc) if xs == "h" else xn_lo(dc))[:, tok0:tok0 + 256]
                        gm = w.op("tensor",
                                  (lambda e, o=gbank[:, 0:256],
                                   l=wgu[sl][:, dc * 128:(dc + 1) * 128], r=rhs,
                                   st=firstg, sp=(xs == "l" and dc == 7):
                                   e.matmul(o, l, r, start=st, stop=sp)),
                                  waits=[t1] + (pbk_cons[sl] if firstg else []),
                                  inc=c_te)
                        firstg = False
                for xs in ("h", "l"):
                    for dc in range(8):
                        rhs = (xn_hi(dc) if xs == "h" else xn_lo(dc))[:, tok0:tok0 + 256]
                        um = w.op("tensor",
                                  (lambda e, o=ubank[:, 0:256],
                                   l=wgu[sl][:, 1024 + dc * 128:1024 + (dc + 1) * 128],
                                   r=rhs,
                                   st=firstu, sp=(xs == "l" and dc == 7):
                                   e.matmul(o, l, r, start=st, stop=sp)),
                                  waits=[t2] + (pbk_cons[2 + sl] if firstu else []),
                                  inc=c_te)
                        firstu = False
                wg_cons[sl] = [gm]
                wu_cons[sl] = [um]
                pbk_cons[sl] = []
                pbk_cons[2 + sl] = []
                s1 = w.op("scalar",
                          (lambda e, d=sil[sl], s_=gbank:
                           e.activation(d[:, :], s_[:, 0:256], AF.Silu,
                                        scale=1.0 / 127.0)),
                          waits=[gm] + sil_cons[sl], inc=c_se)
                pbk_cons[sl].append(s1)
                m1 = w.op("vector",
                          (lambda e, d=hfl[sl], a=sil[sl], s_=ubank:
                           e.scalar_tensor_tensor(d[:, :], s_[:, 0:256],
                                                  1.0 / 127.0, a[:, :],
                                                  AL.mult, AL.mult)),
                          waits=[s1, um] + hfl_cons[sl], inc=c_ve)
                pbk_cons[2 + sl].append(m1)
                sil_cons[sl] = [m1]
                h1 = w.op("scalar",
                          (lambda e, d=hact_hi[:, hc * 256:(hc + 1) * 256],
                           a=hfl[sl]:
                           e.activation(d, a[:, :], AF.Copy)),
                          waits=[m1] + hact_cons, inc=c_se)
                l1 = w.op("vector",
                          (lambda e, d=hact_lo[:, hc * 256:(hc + 1) * 256],
                           a=hfl[sl], h=hact_hi[:, hc * 256:(hc + 1) * 256]:
                           e.tensor_sub(d, a[:, :], h)),
                          waits=[h1], inc=c_ve)
                hfl_cons[sl] = [h1, l1]
                hi_toks.append(h1)
                hi_toks.append(l1)
            dm_last = [None] * 4
            for hc in range(NHC):
                sl = (hc // 2) % 2
                if hc % 2 == 0:
                    ta = w.dma("sync",
                               wdt[sl][:, 0:1024].rearrange("p (oc j) -> p oc j", oc=8),
                               A(scr_wblk, 2 * NHC * 8 * 16384 + hc * 8 * 16384,
                                 [[128, 128], [16384, 8], [1, 128]]),
                               waits=wd_cons[sl], sem=ld_wd[sl])
                    tb2 = w.dma("sync",
                                wdt[sl][:, 1024:2048].rearrange("p (oc j) -> p oc j", oc=8),
                                A(scr_wblk, 2 * NHC * 8 * 16384 + (hc + 1) * 8 * 16384,
                                  [[128, 128], [16384, 8], [1, 128]]),
                                waits=wd_cons[sl], sem=ld_wd[sl])
                    wd_prod[sl] = tb2
                    wd_cons[sl] = []
                t3 = wd_prod[sl]
                co = (hc % 2) * 1024
                dmt = None
                for tt2 in range(2):
                    for oh in range(2):
                        bank = pb[tt2 * 2 + oh]
                        for xs, HX in (("h", hact_hi), ("l", hact_lo)):
                            dmt = w.op("tensor",
                                       (lambda e, o=bank[:, 0:512],
                                        l=HX[:, hc * 256 + tt2 * 128:hc * 256 + tt2 * 128 + 128],
                                        r=wdt[sl][:, co + oh * 512:co + (oh + 1) * 512],
                                        st=(hc == 0 and xs == "h"),
                                        sp=(hc == NHC - 1 and xs == "l"):
                                        e.matmul(o, l, r, start=st, stop=sp)),
                                       waits=([t3, hi_toks[0], hi_toks[1]]
                                              + pbk_cons[tt2 * 2 + oh])
                                       if (hc == 0 and xs == "h") else [t3, hi_toks[2 * hc + (0 if xs == "h" else 1)]],
                                       inc=c_te)
                            dm_last[tt2 * 2 + oh] = dmt
                wd_cons[sl].append(dmt)
            for i in range(4):
                pbk_cons[i] = []
            for tt2 in range(2):
                tt = tb * 2 + tt2
                sg1 = w.op("scalar",
                           (lambda e, s_=pb[tt2 * 2 + 0]:
                            e.activation(gt[:, 0:512], s_[:, 0:512], AF.Silu,
                                         scale=1.0 / 127.0)),
                           waits=[dm_last[tt2 * 2 + 0]] + gt_cons, inc=c_se)
                sg2 = w.op("scalar",
                           (lambda e, s_=pb[tt2 * 2 + 1]:
                            e.activation(gt[:, 512:1024], s_[:, 0:512], AF.Silu,
                                         scale=1.0 / 127.0)),
                           waits=[dm_last[tt2 * 2 + 1]], inc=c_se)
                pbk_cons[tt2 * 2 + 0].append(sg1)
                pbk_cons[tt2 * 2 + 1].append(sg2)
                s = tt2 % 2
                xk = w.dma("sync", xt2[s][:, :],
                           A(x_in, (b * T + tt * 128) * D, [[D, 128], [1, D]]),
                           waits=xt2_cons[s], sem=ld_x2[s])
                xt2_prod[s] = xk
                xt2_cons[s] = []
                o1 = w.op("vector",
                          (lambda e, d=outt[s], tt=tt:
                           e.tensor_mul(d[:, :], s_tile(tt)[:, :], gt[:, :])),
                          waits=[sg1, sg2] + outt_cons[s] + (as_toks if tb == 0 else []),
                          inc=c_ve)
                gt_cons = [o1]
                o2 = w.op("vector",
                          (lambda e, d=outt[s], x_=xt2[s]:
                           e.tensor_add(d[:, :], d[:, :], x_[:, :])),
                          waits=[o1, xk], inc=c_ve)
                xt2_cons[s] = [o2]
                od = w.dma("scalar", A(y_out, (b * T + tt * 128) * D,
                                       [[D, 128], [1, D]]),
                           outt[s][:, :], waits=[o2], sem=st_y[s])
                outt_cons[s] = [od]
        w.fence()

    for e in ("scalar", "sync", "gpsimd"):
        toks = [Tok(s_, s_.count, e, True) for s_ in w.dma_sems[e] if s_.count]
        w.op(e, (lambda en: en.nop()), waits=toks)

    w.emit()
    ctx.close()
    return nc


_NC = None


def _get_nc():
    global _NC
    if _NC is None:
        _NC = build_program()
    return _NC


def kernel(L0_attn_x, L0_attn_Wq, L0_attn_bq, L0_attn_Wk, L0_attn_bk,
           L0_attn_Wv, L0_attn_bv, L0_attn_Wo, L0_attn_bo,
           L0_attn_gamma, L0_attn_beta,
           L0_ffn_w_gate, L0_ffn_w_up, L0_ffn_w_down):
    x = np.ascontiguousarray(np.asarray(L0_attn_x, dtype=np.float32))
    tri = np.triu(np.ones((128, 128), dtype=np.float32))
    maskbig = ((tri - 1.0) * 200.0).astype(ml_dtypes.bfloat16)
    identb = np.eye(128, dtype=ml_dtypes.bfloat16)
    identf = np.eye(128, dtype=np.float32)
    base = dict(
        wq=np.ascontiguousarray(np.asarray(L0_attn_Wq, dtype=np.float32)),
        wk=np.ascontiguousarray(np.asarray(L0_attn_Wk, dtype=np.float32)),
        wv=np.ascontiguousarray(np.asarray(L0_attn_Wv, dtype=np.float32)),
        wo=np.ascontiguousarray(np.asarray(L0_attn_Wo, dtype=np.float32)),
        bq=np.ascontiguousarray(np.asarray(L0_attn_bq, dtype=np.float32)),
        bk=np.ascontiguousarray(np.asarray(L0_attn_bk, dtype=np.float32)),
        bv=np.ascontiguousarray(np.asarray(L0_attn_bv, dtype=np.float32)),
        bo=np.ascontiguousarray(np.asarray(L0_attn_bo, dtype=np.float32)),
        gamma=np.ascontiguousarray(np.asarray(L0_attn_gamma, dtype=np.float32)),
        beta=np.ascontiguousarray(np.asarray(L0_attn_beta, dtype=np.float32)),
        wg=np.ascontiguousarray(np.asarray(L0_ffn_w_gate, dtype=np.int32)),
        wu=np.ascontiguousarray(np.asarray(L0_ffn_w_up, dtype=np.int32)),
        wd=np.ascontiguousarray(np.asarray(L0_ffn_w_down, dtype=np.int32)),
        tri=tri, maskbig=maskbig, identb=identb, identf=identf,
    )
    in_maps = []
    for c in range(8):
        m = dict(base)
        m["x"] = np.ascontiguousarray(x[2 * c:2 * c + 2])
        in_maps.append(m)
    nc = _get_nc()
    res = run_bass_kernel_spmd(nc, in_maps, core_ids=list(range(8)))
    out = np.concatenate([r["y"].reshape(2, T, D) for r in res.results], axis=0)
    return out.astype(np.float32)
